# version 10
# baseline (speedup 1.0000x reference)
import numpy as np
import ml_dtypes
import concourse.bass as bass
import concourse.mybir as mybir
from concourse.bass_utils import run_bass_kernel_spmd

F32 = mybir.dt.float32
BF16 = mybir.dt.bfloat16
AF = mybir.ActivationFunctionType
ALU = mybir.AluOpType
AX = mybir.AxisListType

T_TILE = 1024
NCHUNK = 8
C = 128
H = 1024
NH = 32
NE = 18
EGRP = 3
LN_EPS = 1e-5
ATT_EPS = 1e-6
BMIN, BMAX = 0.85, 0.9995

ENGINES = ("tensor", "vector", "scalar", "sync", "gpsimd")

ARENA_TAGS = {
    "wq_s", "wk_s", "wv_s", "wo_s", "wbm_s", "a_tm", "a_bf", "a_fm", "qke", "rr",
    "tmp", "phiq_tm", "phik_tm", "phiq_fm", "phik_fm", "vy", "bm_sb", "ebx",
    "rbeta", "invb_fm", "mix_fm", "mw", "invb_tm", "mix_tm", "epsinvb", "bC_col",
    "dsb2", "rsb2", "wsb2",
    "bC_row", "bCb", "vsaug", "ptb", "z_tm", "aff", "tl2", "tlz", "tlfm",
    "routed", "h", "gww0", "gww1", "uww0", "uww1", "dww0", "dww1", "dww2",
    "s_sb", "bcomb", "sc_all", "scores_a", "gsw", "m8a", "m8b",
    "masked", "smask", "ohk", "cand", "wk4",
    "wks", "comb2", "comb", "combfm", "crow", "r_tm", "ya_sb",
}


class Prog:

    def __init__(self):
        self.ops = {e: [] for e in ENGINES}
        self.last_w = {}
        self.readers = {}

    def emit(self, eng, fn, reads=(), writes=(), dma=False, force_mark=False):
        reads = list(reads)
        if any(k[0] in ARENA_TAGS for k in list(reads) + list(writes)):
            reads.append(("AR",))
        idx = len(self.ops[eng])
        deps = set()
        for k in reads:
            w = self.last_w.get(k)
            if w is not None:
                deps.add(w)
            self.readers.setdefault(k, []).append((eng, idx))
        for k in writes:
            w = self.last_w.get(k)
            if w is not None:
                deps.add(w)
            for r in self.readers.get(k, ()):
                deps.add(r)
            self.readers[k] = []
            self.last_w[k] = (eng, idx)
        final = []
        sdeps = []
        for (e2, i2) in deps:
            if (e2, i2) == (eng, idx):
                continue
            if e2 == eng and not self.ops[e2][i2][2]:
                if eng == "vector" and idx - i2 < 8:
                    final.append((e2, i2))
                else:
                    sdeps.append(i2)
                continue
            final.append((e2, i2))
        self.ops[eng].append([fn, final, dma, force_mark, None, sdeps])

    def guard(self):
        self.emit("vector", lambda E: E.engine_nop(), reads=[], writes=[("AR",)])

    def _insert_vector_nops(self):
        old = self.ops["vector"]
        new = []
        newpos = []
        self.n_nops = 0
        for i, op in enumerate(old):
            dmax = max(op[5], default=-1)
            if dmax >= 0 and newpos[dmax] == len(new) - 1:
                new.append([lambda E: E.tensor_copy(out=self._jnk[:, 1:2],
                                                    in_=self._jnk[:, 0:1]),
                            [], False, False, None, []])
                self.n_nops += 1
            newpos.append(len(new))
            new.append(op)
        self.ops["vector"] = new
        for e in ENGINES:
            for op in self.ops[e]:
                op[1] = [(e2, newpos[i2] if e2 == "vector" else i2) for (e2, i2) in op[1]]

    def finalize(self):
        for e in ENGINES:
            for op in self.ops[e]:
                for (e2, i2) in op[1]:
                    self.ops[e2][i2][3] = True
        cum = {}
        for e in ENGINES:
            cum[e] = []
            v = 0
            for op in self.ops[e]:
                if op[3]:
                    v += 16 if op[2] else 1
                cum[e].append(v)
        know = {e: {e2: 0 for e2 in ENGINES} for e in ENGINES}
        nwaits = 0
        for e in ENGINES:
            for op in self.ops[e]:
                waits = {}
                for (e2, i2) in op[1]:
                    tv = cum[e2][i2]
                    if know[e][e2] < tv:
                        waits[e2] = max(waits.get(e2, 0), tv)
                for e2, tv in waits.items():
                    know[e][e2] = tv
                op[4] = sorted(waits.items())
                nwaits += len(op[4])
        self.cum = cum
        self.nwaits = nwaits

    def total(self, e):
        return self.cum[e][-1] if self.ops[e] else 0


def _prep_host(inputs):
    f32 = np.float32
    bf16 = ml_dtypes.bfloat16
    d = {}
    d["wqT"] = np.ascontiguousarray(inputs["L0_attn_Wq"].T, dtype=f32)
    d["wkT"] = np.ascontiguousarray(inputs["L0_attn_Wk"].T, dtype=f32)
    d["wvT"] = np.ascontiguousarray(inputs["L0_attn_Wv"].T, dtype=f32)
    d["woT"] = np.ascontiguousarray(inputs["L0_attn_Wout"].T, dtype=f32)
    perm = np.array([(ch % 32) * 2 + (ch // 32) for ch in range(64)])
    wb = np.asarray(inputs["L0_attn_Wbeta"])[perm]
    wm = np.asarray(inputs["L0_attn_Wmix"])[perm]
    d["wbmT"] = np.ascontiguousarray(np.concatenate([wb, wm], 0).T, dtype=f32)
    d["bbm_col"] = np.concatenate([np.asarray(inputs["L0_attn_bbeta"])[perm],
                                   np.asarray(inputs["L0_attn_bmix"])[perm]]).reshape(128, 1).astype(f32)
    d["bout_row"] = np.asarray(inputs["L0_attn_bout"]).reshape(1, 1024).astype(bf16)
    d["lng_b"] = np.ascontiguousarray(np.broadcast_to(inputs["L0_attn_ln_g"], (128, 1024)), dtype=bf16)
    d["lnb_b"] = np.ascontiguousarray(np.broadcast_to(inputs["L0_attn_ln_b"], (128, 1024)), dtype=bf16)
    d["wrT"] = np.ascontiguousarray(inputs["L0_ffn_router_weight"].T, dtype=f32)
    d["ebias_b"] = np.ascontiguousarray(np.broadcast_to(inputs["L0_ffn_e_bias"], (128, 16)), dtype=f32)
    gw, uw, dw = (np.asarray(inputs["L0_ffn_gate_w"]), np.asarray(inputs["L0_ffn_up_w"]),
                  np.asarray(inputs["L0_ffn_down_w"]))
    shgT = np.asarray(inputs["L0_ffn_sh_gate_w"]).T
    shuT = np.asarray(inputs["L0_ffn_sh_up_w"]).T
    shdT = np.asarray(inputs["L0_ffn_sh_down_w"]).T
    gts = [gw[e].T for e in range(16)] + [shgT[:, 0:512], shgT[:, 512:1024]]
    uts = [uw[e].T for e in range(16)] + [shuT[:, 0:512], shuT[:, 512:1024]]
    dts = [dw[e].T for e in range(16)] + [shdT[0:512], shdT[512:1024]]
    d["gwT"] = np.ascontiguousarray(np.stack(gts), dtype=f32)
    d["uwT"] = np.ascontiguousarray(np.stack(uts), dtype=f32)
    d["dwT"] = np.ascontiguousarray(np.stack(dts), dtype=f32)
    half = 16
    inv_freq = 1.0 / 10000.0 ** (np.arange(half, dtype=np.float64) / half)
    freqs = np.arange(2048, dtype=np.float64)[:, None] * inv_freq
    d["cosrep"] = np.tile(np.cos(freqs), (1, 32)).astype(bf16)
    d["sinrep"] = np.tile(np.sin(freqs), (1, 32)).astype(bf16)
    d["triu"] = np.triu(np.ones((128, 128), f32))
    d["ident_bf"] = np.eye(128, dtype=bf16)
    d["ident_f32"] = np.eye(128, dtype=f32)
    d["iota16"] = np.broadcast_to(np.arange(16, dtype=f32), (128, 16)).copy()
    d["iota4"] = np.broadcast_to(np.arange(4, dtype=f32), (128, 4)).copy()
    d["ones_f32"] = np.ones((1, 128), f32)
    d["ones_bf"] = np.ones((1, 128), bf16)
    sd = np.zeros((128, 32), f32)
    for hh in range(32):
        sd[64 + hh, hh] = 1.0
        sd[96 + hh, hh] = -1.0
    d["seldiff"] = sd
    return d


def build_nc(n_tiles=4, dbg=(), stage=4, sub=9, nosd=False):
    nc = bass.Bass("TRN2", target_bir_lowering=False, detect_race_conditions=False)
    T_core = n_tiles * T_TILE

    X = nc.dram_tensor("x", [T_core, H], F32, kind="ExternalInput").ap()
    OUT = nc.dram_tensor("out", [T_core, H], F32, kind="ExternalOutput").ap()
    dram = {}
    for nm, shp, dt in [
        ("wqT", [H, H], F32), ("wkT", [H, H], F32), ("wvT", [H, H], F32),
        ("woT", [H, H], F32), ("wbmT", [H, 128], F32), ("bbm_col", [128, 1], F32),
        ("bout_row", [1, H], BF16), ("lng_b", [128, H], BF16), ("lnb_b", [128, H], BF16),
        ("wrT", [H, 16], F32), ("ebias_b", [128, 16], F32),
        ("gwT", [NE, H, 512], F32), ("uwT", [NE, H, 512], F32), ("dwT", [NE, 512, H], F32),
        ("cosrep", [2048, 512], BF16), ("sinrep", [2048, 512], BF16),
        ("triu", [128, 128], F32), ("ident_bf", [128, 128], BF16),
        ("ident_f32", [128, 128], F32), ("iota16", [128, 16], F32),
        ("iota4", [128, 4], F32), ("ones_f32", [1, 128], F32), ("ones_bf", [1, 128], BF16),
        ("seldiff", [128, 32], F32),
    ]:
        dram[nm] = nc.dram_tensor(nm, shp, dt, kind="ExternalInput").ap()
    dbg_t = {nm: nc.dram_tensor(nm, shp, dt, kind="ExternalOutput").ap()
             for nm, shp, dt in dbg}

    sb = {}

    def S(name, shape, dt):
        sb[name] = nc.alloc_sbuf_tensor(name, shape, dt)
        return sb[name]

    x_sb = S("x_sb", [128, NCHUNK * H], F32)
    S_all = S("S_state", [128, NH * 66], F32)
    S("S_bf", [128, NH * 66], BF16)
    z1_fm = S("z1_fm", [128, 8 * T_TILE], BF16)
    S("triu_s", [128, 128], F32)
    S("ident_f32_s", [128, 128], F32)
    S("ident_bf_s", [128, 128], BF16)
    S("lng_s", [128, H], BF16)
    S("lnb_s", [128, H], BF16)
    S("iota16_s", [128, 16], F32)
    S("iota4_s", [128, 4], F32)
    S("ebias_s", [128, 16], F32)
    S("bbm_s", [128, 1], F32)
    S("ones_f32_s", [1, 128], F32)
    S("seldiff_s", [128, 32], F32)
    S("ones_bf_s", [1, 128], BF16)
    S("bout_s", [1, H], BF16)
    S("wr_s", [128, 8 * 16], BF16)
    S("one64", [64, 128], F32)
    S("csl0", [128, 1024], BF16)
    S("csl1", [128, 1024], BF16)
    S("stat", [128, 64], F32)
    S("e_all", [128, 128], F32)
    sb["e_all"] = sb["e_all"][:]
    S("scr", [128, 1024], BF16)

    AR = S("arena", [128, 66688], BF16)

    def carve(cur, name, n_elems, dt, parts=128):
        off = cur // 2
        if dt == BF16:
            ap = AR[0:parts, off:off + n_elems]
            nxt = cur + 2 * n_elems
        else:
            ap = AR[0:parts, off:off + 2 * n_elems].bitcast(F32)
            nxt = cur + 4 * n_elems
        assert nxt % 32 == 0 or True
        sb[name] = ap
        return nxt

    cur = 0
    for nm, n, dt in [
        ("wq_s", 8192, BF16), ("wk_s", 8192, BF16), ("wv_s", 8192, BF16),
        ("wo_s", 8192, BF16), ("wbm_s", 1024, BF16),
        ("a_tm", 1024, F32), ("a_bf", 1024, BF16), ("a_fm", 1024, BF16),
        ("qke", 1024, BF16), ("rr", 1024, BF16), ("tmp_bf", 1024, BF16),
        ("phiq_tm", 1024, BF16), ("phik_tm", 1024, BF16),
        ("phiq_fm", 1024, BF16), ("phik_fm", 1024, BF16),
        ("vy", 1024, F32), ("bm_sb", 128, F32),
        ("ptb0", 512, BF16), ("ptb1", 512, BF16),
        ("vsaug", 2112, BF16),
        ("invb_tm", 64, F32), ("mix_tm", 64, F32), ("epsinvb", 64, F32),
        ("bC_row", 64, F32), ("bCb", 64, F32),
        ("dsb2", 64, F32), ("rsb2", 64, F32), ("wsb2", 64, F32),
        ("z_tm", 1024, F32), ("aff", 1024, F32),
        ("tl2_bf", 1024, BF16), ("tl_bf", 1024, BF16), ("tl_fm", 1024, BF16),
        ("ya_sb", 2112, F32),
    ]:
        cur = carve(cur, nm, n, dt)
    for nm, n in [("ebx", 128), ("rbeta", 128), ("invb_fm", 128),
                  ("mix_fm", 128), ("mw", 128)]:
        cur = carve(cur, nm, n, F32, parts=64)
    cur = carve(cur, "bC_col", 8, F32, parts=64)
    attn_end = cur

    cur = 0
    for nm, n, dt in [
        ("routed", 8192, F32),
        ("h0", 4096, BF16), ("h1", 4096, BF16), ("h2", 4096, BF16),
        ("gww0", 4096, BF16), ("gww1", 4096, BF16),
        ("uww0", 4096, BF16), ("uww1", 4096, BF16),
        ("dww0", 4096, BF16), ("dww1", 4096, BF16), ("dww2", 4096, BF16),
        ("s_sb", 1024, BF16), ("bcomb", 1024, BF16),
        ("sc_all", 128, F32), ("scores_a", 128, F32),
        ("gsw", 256, F32), ("m8a", 16, F32), ("m8b", 16, F32),
        ("masked", 128, F32), ("smask", 128, F32), ("ohk", 128, F32), ("cand", 128, F32),
        ("wk4", 32, F32), ("combA", 128, F32), ("combB", 128, F32),
        ("comb_all", 128, F32), ("r_tm", 1024, F32),
    ]:
        cur = carve(cur, nm, n, dt)
    cur = carve(cur, "comb_fm", 1024, BF16, parts=16)
    cur = carve(cur, "crow", 1024, BF16, parts=1)
    moe_end = cur
    assert max(attn_end, moe_end) <= 2 * 66688, (attn_end, moe_end)

    ps = {i: nc.alloc_psum_tensor(f"ps{i}", [128, 512], F32) for i in range(6)}
    ps[6] = nc.alloc_psum_tensor("ps6", [128, 1024], BF16)
    ps[7] = nc.alloc_psum_tensor("ps7", [128, 512], F32)

    P = Prog()
    P._jnk = S("jnk", [128, 2], F32)

    def dma(eng, out, in_, reads, writes):
        P.emit(eng, lambda E, o=out, i=in_: E.dma_start(out=o, in_=i),
               reads=reads, writes=writes, dma=True, force_mark=True)

    def mm(out, lhsT, rhs, start, stop, reads, writes, tp=None):
        P.emit("tensor",
               lambda E, o=out, l=lhsT, r=rhs, st=start, sp=stop, t=tp:
               E.matmul(o, l, r, start=st, stop=sp, tile_position=t),
               reads=reads, writes=writes)

    def trans(out, in_, ident, reads, writes):
        P.emit("tensor", lambda E, o=out, i=in_, d=ident: E.transpose(o, i, d),
               reads=reads, writes=writes)

    def act(out, in_, func, reads, writes, bias=0.0, scale=1.0, accum_out=None):
        P.emit("scalar", lambda E, o=out, i=in_, f=func, b=bias, s=scale, a=accum_out:
               E.activation(out=o, in_=i, func=f, bias=b, scale=s, accum_out=a),
               reads=reads, writes=writes)

    def tt(eng, out, in0, in1, op, reads, writes):
        P.emit(eng, lambda E, o=out, a=in0, b=in1, p=op:
               E.tensor_tensor(out=o, in0=a, in1=b, op=p), reads=reads, writes=writes)

    def ts(eng, out, in0, s1, op0, reads, writes, s2=None, op1=ALU.bypass):
        P.emit(eng, lambda E, o=out, a=in0, x=s1, y=s2, p0=op0, p1=op1:
               E.tensor_scalar(out=o, in0=a, scalar1=x, scalar2=y, op0=p0, op1=p1),
               reads=reads, writes=writes)

    def stt(eng, out, in0, scalar, in1, op0, op1, reads, writes):
        P.emit(eng, lambda E, o=out, a=in0, s=scalar, b=in1, p0=op0, p1=op1:
               E.scalar_tensor_tensor(out=o, in0=a, scalar=s, in1=b, op0=p0, op1=p1),
               reads=reads, writes=writes)

    def cp(eng, out, in_, reads, writes):
        P.emit(eng, lambda E, o=out, i=in_: E.tensor_copy(out=o, in_=i),
               reads=reads, writes=writes)

    def red(out, in_, op, reads, writes):
        P.emit("vector", lambda E, o=out, i=in_, p=op: E.tensor_reduce(o, i, AX.X, p),
               reads=reads, writes=writes)

    def recip(out, in_, reads, writes):
        P.emit("vector", lambda E, o=out, i=in_: E.reciprocal(out=o, in_=i),
               reads=reads, writes=writes)

    def mset(eng, ap_, val, reads, writes):
        P.emit(eng, lambda E, a=ap_, v=val: E.memset(a, v), reads=reads, writes=writes)

    def scan(out, d0, d1, init, op0, op1, reads, writes):
        P.emit("vector", lambda E, o=out, a=d0, b=d1, i=init, p0=op0, p1=op1:
               E.tensor_tensor_scan(out=o, data0=a, data1=b, initial=i, op0=p0, op1=p1),
               reads=reads, writes=writes)

    def dbg_store(name, src_ap, reads):
        if name in dbg_t:
            dma("sync", dbg_t[name], src_ap, reads=reads, writes=[("dbg", name)])

    st = sb["stat"]

    def ln_stats(src_ap, src_keys, col, n=1024):
        act(sb["scr"][:, 0:n], src_ap, AF.Identity, reads=src_keys,
            writes=[("scr",), ("st", col)], accum_out=st[:, col:col + 1])
        act(sb["scr"][:, 0:n], src_ap, AF.Square, reads=src_keys,
            writes=[("scr",), ("st", col + 1)], accum_out=st[:, col + 1:col + 2])
        ts("vector", st[:, col:col + 1], st[:, col:col + 1], 1.0 / n, ALU.mult,
           reads=[("st", col)], writes=[("st", col)])
        ts("vector", st[:, col + 1:col + 2], st[:, col + 1:col + 2], 1.0 / n, ALU.mult,
           reads=[("st", col + 1)], writes=[("st", col + 1)])
        stt("vector", st[:, col + 2:col + 3], st[:, col:col + 1], st[:, col:col + 1],
            st[:, col + 1:col + 2], ALU.mult, ALU.subtract,
            reads=[("st", col), ("st", col + 1)], writes=[("st", col + 2)])
        ts("vector", st[:, col + 2:col + 3], st[:, col + 2:col + 3], -1.0, ALU.mult,
           reads=[("st", col + 2)], writes=[("st", col + 2)], s2=LN_EPS, op1=ALU.add)
        act(st[:, col + 3:col + 4], st[:, col + 2:col + 3], AF.Ln,
            reads=[("st", col + 2)], writes=[("st", col + 3)])
        act(st[:, col + 2:col + 3], st[:, col + 3:col + 4], AF.Exp,
            reads=[("st", col + 3)], writes=[("st", col + 2)], scale=-0.5)
        stt("vector", st[:, col + 3:col + 4], st[:, col:col + 1], -1.0,
            st[:, col + 2:col + 3], ALU.mult, ALU.mult,
            reads=[("st", col), ("st", col + 2)], writes=[("st", col + 3)])
        return st[:, col + 2:col + 3], st[:, col + 3:col + 4]

    for nm, tgt in [("triu", "triu_s"), ("ident_f32", "ident_f32_s"), ("lng_b", "lng_s"),
                    ("lnb_b", "lnb_s"), ("ident_bf", "ident_bf_s"), ("iota16", "iota16_s"),
                    ("iota4", "iota4_s"), ("ebias_b", "ebias_s"), ("bbm_col", "bbm_s"),
                    ("ones_f32", "ones_f32_s"), ("ones_bf", "ones_bf_s"), ("bout_row", "bout_s"),
                    ("seldiff", "seldiff_s")]:
        dma("sync", sb[tgt][:], dram[nm], reads=[], writes=[(tgt,)])
    dma("gpsimd", sb["wr_s"][:].rearrange("p (j f) -> p j f", f=16),
        dram["wrT"].rearrange("(j p) f -> p j f", p=128), reads=[], writes=[("wr_s",)])
    mset("vector", sb["one64"][:], 1.0, reads=[], writes=[("one64",)])

    IBF = sb["ident_bf_s"][:]
    IF = sb["ident_f32_s"][:]
    x_keys = [("x_sb", c) for c in range(NCHUNK)]

    for t in range(n_tiles):
        seq_pos0 = (t % 2) * T_TILE

        for wnm, wsn in [("wqT", "wq_s"), ("wkT", "wk_s"), ("wvT", "wv_s"), ("woT", "wo_s")]:
            dma("gpsimd", sb[wsn].rearrange("p (j f) -> p j f", f=1024),
                dram[wnm].rearrange("(j p) f -> p j f", p=128), reads=[], writes=[(wsn,)])
        dma("gpsimd", sb["wbm_s"].rearrange("p (j f) -> p j f", f=128),
            dram["wbmT"].rearrange("(j p) f -> p j f", p=128), reads=[], writes=[("wbm_s",)])

        dma("sync", x_sb[:].rearrange("p (c f) -> p c f", f=H),
            X[t * T_TILE:(t + 1) * T_TILE, :].rearrange("(c p) f -> p c f", p=128),
            reads=[], writes=x_keys)

        dbg_store(f"dbg_xin_{t}", x_sb[:, 0:1024], [("x_sb", 0)])
        if t % 2 == 0:
            mset("vector", S_all[:], 0.0, reads=[], writes=[("S",)])

        def stage_A(c):
            xc = x_sb[:, c * H:(c + 1) * H]
            xck = ("x_sb", c)
            pos0 = seq_pos0 + c * C
            csl = sb["csl0"] if c % 2 == 0 else sb["csl1"]
            ck = "csl0" if c % 2 == 0 else "csl1"
            dma("sync", csl[:, 0:512], dram["cosrep"][pos0:pos0 + 128, :],
                reads=[], writes=[(ck, 0)])
            dma("sync", csl[:, 512:1024], dram["sinrep"][pos0:pos0 + 128, :],
                reads=[], writes=[(ck, 1)])
            rstd, nbias = ln_stats(xc, [xck], 0)
            dbg_store(f"dbg_st_{t}_{c}", st[:, 0:4], [("st", 0), ("st", 1), ("st", 2), ("st", 3)])
            act(sb["a_tm"][:], xc, AF.Identity, reads=[xck, ("st", 2), ("st", 3)],
                writes=[("a_tm",)], bias=nbias, scale=rstd)
            cp("vector", sb["a_bf"][:], sb["a_tm"][:], reads=[("a_tm",)], writes=[("a_bf",)])
            for j in range(8):
                trans(ps[6][:, j * 128:(j + 1) * 128], sb["a_bf"][:, j * 128:(j + 1) * 128],
                      IBF, reads=[("a_bf",), ("ident_bf_s",)], writes=[("psb", 6)])
            cp("vector", sb["a_fm"][:], ps[6][:], reads=[("psb", 6)], writes=[("a_fm",)])
            dbg_store(f"dbg_a_{t}_{c}", sb["a_tm"][:], [("a_tm",)])

        def stage_B(c):
            xc = x_sb[:, c * H:(c + 1) * H]
            xck = ("x_sb", c)
            csl = sb["csl0"] if c % 2 == 0 else sb["csl1"]
            ck = "csl0" if c % 2 == 0 else "csl1"
            COS = csl[:, 0:512].rearrange("p (h d) -> p h d", d=16)
            SIN = csl[:, 512:1024].rearrange("p (h d) -> p h d", d=16)
            for (wsn, b0) in [("wq_s", 0), ("wk_s", 2), ("wv_s", 4)]:
                for th in range(2):
                    for j in range(8):
                        mm(ps[b0 + th][:, 0:512],
                           sb["a_fm"][:, j * 128:(j + 1) * 128],
                           sb[wsn][:, j * 1024 + th * 512: j * 1024 + (th + 1) * 512],
                           start=(j == 0), stop=(j == 7),
                           reads=[("a_fm",), (wsn,)], writes=[("psb", b0 + th)])
            for j in range(8):
                mm(ps[7][:, 0:128], sb["wbm_s"][:, j * 128:(j + 1) * 128],
                   sb["a_fm"][:, j * 128:(j + 1) * 128],
                   start=(j == 0), stop=(j == 7),
                   reads=[("a_fm",), ("wbm_s",)], writes=[("psb", 7)])
            act(sb["bm_sb"][:], ps[7][:, 0:128], AF.Identity,
                reads=[("psb", 7), ("bbm_s",)], writes=[("bm_sb",)], bias=sb["bbm_s"][:, 0:1])
            if sub < 2:
                return False

            act(sb["ebx"], sb["bm_sb"][0:64, :], AF.Exp, reads=[("bm_sb",)],
                writes=[("ebx",)], scale=-1.0)
            ts("vector", sb["rbeta"], sb["ebx"], 1.0, ALU.add,
               reads=[("ebx",)], writes=[("rbeta",)], s2=1.0 / BMAX, op1=ALU.max)
            ts("vector", sb["rbeta"], sb["rbeta"], 1.0 / BMIN, ALU.min,
               reads=[("rbeta",)], writes=[("rbeta",)])
            scan(sb["invb_fm"], sb["rbeta"], sb["one64"][:], 1.0, ALU.mult, ALU.mult,
                 reads=[("rbeta",), ("one64",)], writes=[("invb_fm",)])
            recip(sb["bC_col"][:, 0:1], sb["invb_fm"][:, 127:128],
                  reads=[("invb_fm",)], writes=[("bC_col",)])

            mm(ps[7][0:32, 384:512], sb["seldiff_s"][:], sb["bm_sb"][:],
               start=True, stop=True, reads=[("seldiff_s",), ("bm_sb",)],
               writes=[("psb", 7)])
            act(sb["mw"][0:32, :], ps[7][0:32, 384:512], AF.Exp, reads=[("psb", 7)],
                writes=[("mw", 0)], scale=-1.0)
            ts("vector", sb["mw"][0:32, :], sb["mw"][0:32, :], 1.0, ALU.add,
               reads=[("mw", 0)], writes=[("mw", 0)])
            recip(sb["mix_fm"][0:32, :], sb["mw"][0:32, :], reads=[("mw", 0)],
                  writes=[("mix_fm",)])

            trans(ps[7][:, 128:192], sb["invb_fm"], IF[0:64, 0:64],
                  reads=[("invb_fm",), ("ident_f32_s",)], writes=[("psb", 7)])
            trans(ps[7][:, 192:224], sb["mix_fm"][0:32, :], IF[0:32, 0:32],
                  reads=[("mix_fm",), ("ident_f32_s",)], writes=[("psb", 7)])
            trans(ps[7][0:1, 256:320], sb["bC_col"][:, 0:1], IF[0:64, 0:64],
                  reads=[("bC_col",), ("ident_f32_s",)], writes=[("psb", 7)])
            cp("vector", sb["invb_tm"], ps[7][:, 128:192], reads=[("psb", 7)],
               writes=[("invb_tm",)])
            cp("vector", sb["mix_tm"][:, 0:32], ps[7][:, 192:224], reads=[("psb", 7)],
               writes=[("mix_tm",)])
            ts("vector", sb["mix_tm"][:, 32:64], sb["mix_tm"][:, 0:32], -1.0, ALU.mult,
               reads=[("mix_tm",)], writes=[("mix_tm",)], s2=1.0, op1=ALU.add)
            cp("vector", sb["bC_row"][0:1, :], ps[7][0:1, 256:320], reads=[("psb", 7)],
               writes=[("bC_row",)])
            mm(ps[7][:, 320:384], sb["ones_f32_s"][0:1, 0:128], sb["bC_row"][0:1, :],
               start=True, stop=True, reads=[("bC_row",), ("ones_f32_s",)],
               writes=[("psb", 7)])
            cp("vector", sb["bCb"], ps[7][:, 320:384], reads=[("psb", 7)], writes=[("bCb",)])
            ts("vector", sb["epsinvb"], sb["invb_tm"], ATT_EPS, ALU.mult,
               reads=[("invb_tm",)], writes=[("epsinvb",)])
            if sub < 3:
                return False

            for (b0, pnm) in [(0, "phiq_tm"), (2, "phik_tm")]:
                cp("vector", sb["qke"][:, 0:512], ps[b0][:, 0:512],
                   reads=[("psb", b0)], writes=[("qke",)])
                cp("vector", sb["qke"][:, 512:1024], ps[b0 + 1][:, 0:512],
                   reads=[("psb", b0 + 1)], writes=[("qke",)])
                E3 = sb["qke"].rearrange("p (h d) -> p h d", d=32)
                R3 = sb["rr"].rearrange("p (h d) -> p h d", d=32)
                T3 = sb["tmp_bf"].rearrange("p (h d) -> p h d", d=32)
                u1, u2 = E3[:, :, 0:16], E3[:, :, 16:32]
                tt("vector", T3[:, :, 0:16], u1, COS, ALU.mult,
                   reads=[("qke",), (ck, 0)], writes=[("tmp", 0)])
                tt("vector", T3[:, :, 16:32], u2, SIN, ALU.mult,
                   reads=[("qke",), (ck, 1)], writes=[("tmp", 1)])
                tt("vector", R3[:, :, 0:16], T3[:, :, 0:16], T3[:, :, 16:32], ALU.subtract,
                   reads=[("tmp", 0), ("tmp", 1)], writes=[("rr",)])
                tt("vector", T3[:, :, 0:16], u1, SIN, ALU.mult,
                   reads=[("qke",), (ck, 1)], writes=[("tmp", 0)])
                tt("vector", T3[:, :, 16:32], u2, COS, ALU.mult,
                   reads=[("qke",), (ck, 0)], writes=[("tmp", 1)])
                tt("vector", R3[:, :, 16:32], T3[:, :, 0:16], T3[:, :, 16:32], ALU.add,
                   reads=[("tmp", 0), ("tmp", 1)], writes=[("rr",)])
                ts("vector", sb["tmp_bf"], sb["rr"], 0.0, ALU.min,
                   reads=[("rr",)], writes=[("tmp", 0), ("tmp", 1)])
                act(sb["tmp_bf"], sb["tmp_bf"], AF.Exp,
                    reads=[("tmp", 0), ("tmp", 1)], writes=[("tmp", 0), ("tmp", 1)])
                ts("vector", sb["qke"], sb["rr"], 0.0, ALU.max,
                   reads=[("rr",)], writes=[("qke",)])
                tt("vector", sb[pnm], sb["tmp_bf"], sb["qke"], ALU.add,
                   reads=[("tmp", 0), ("tmp", 1), ("qke",)], writes=[(pnm,)])

            cp("vector", sb["vy"][:, 0:512], ps[4][:, 0:512], reads=[("psb", 4)],
               writes=[("vy",)])
            cp("vector", sb["vy"][:, 512:1024], ps[5][:, 0:512], reads=[("psb", 5)],
               writes=[("vy",)])

            for (pnm, fnm) in [("phiq_tm", "phiq_fm"), ("phik_tm", "phik_fm")]:
                for j in range(8):
                    trans(ps[6][:, j * 128:(j + 1) * 128], sb[pnm][:, j * 128:(j + 1) * 128],
                          IBF, reads=[(pnm,), ("ident_bf_s",)], writes=[("psb", 6)])
                cp("vector", sb[fnm], ps[6][:], reads=[("psb", 6)], writes=[(fnm,)])

            dbg_store(f"dbg_pqf_{t}_{c}", sb["phiq_fm"], [("phiq_fm",)])
            dbg_store(f"dbg_pkf_{t}_{c}", sb["phik_fm"], [("phik_fm",)])
            dbg_store(f"dbg_pqt_{t}_{c}", sb["phiq_tm"], [("phiq_tm",)])
            VS4 = sb["vsaug"].rearrange("p (h m e) -> p h m e", m=2, e=33)
            IV3 = sb["invb_tm"].rearrange("p (m h) -> p h m", h=32)
            tt("vector", VS4[:, :, :, 0:32],
               sb["vy"].rearrange("p (h d) -> p h d", d=32).unsqueeze(2).to_broadcast([128, 32, 2, 32]),
               IV3.unsqueeze(3).to_broadcast([128, 32, 2, 32]),
               ALU.mult, reads=[("vy",), ("invb_tm",)], writes=[("vsaug",)])
            cp("vector", VS4[:, :, :, 32:33], IV3.unsqueeze(3),
               reads=[("invb_tm",)], writes=[("vsaug",)])

            dbg_store(f"dbg_vsaug_{t}_{c}", sb["vsaug"], [("vsaug",)])
            cp("vector", sb["S_bf"][:], S_all[:], reads=[("S",)], writes=[("S_bf",)])
            if sub < 4:
                return False

            for g in range(8):
                ptsb = sb["ptb0"] if g % 2 == 0 else sb["ptb1"]
                ptk = ("ptb", g % 2)
                yaps = ps[g % 2]
                yak = ("psb", g % 2)
                for w in range(2):
                    for k in (2 * w, 2 * w + 1):
                        mm(ps[2 + k % 2][:, 0:128],
                           sb["phik_fm"][:, 128 * g:128 * (g + 1)][32 * k:32 * k + 32, :],
                           sb["phiq_fm"][:, 128 * g:128 * (g + 1)][32 * k:32 * k + 32, :],
                           start=True, stop=True, tp=(32 * k, 0),
                           reads=[("phik_fm",), ("phiq_fm",)], writes=[("psb", 2 + k % 2)])
                    for k in (2 * w, 2 * w + 1):
                        stt("vector", ptsb[:, 128 * k:128 * (k + 1)],
                            ps[2 + k % 2][:, 0:128], 1.0, sb["triu_s"][:],
                            ALU.mult, ALU.mult,
                            reads=[("psb", 2 + k % 2), ("triu_s",)], writes=[ptk])
                for k in range(4):
                    h = 4 * g + k
                    mm(yaps[:, 66 * k:66 * (k + 1)], ptsb[:, 128 * k:128 * (k + 1)],
                       sb["vsaug"][:, 66 * h:66 * (h + 1)],
                       start=True, stop=False, reads=[ptk, ("vsaug",)], writes=[yak])
                    mm(yaps[:, 66 * k:66 * (k + 1)], sb["phiq_fm"][:, 128 * g:128 * (g + 1)],
                       sb["S_bf"][:, 66 * h:66 * (h + 1)],
                       start=False, stop=True, reads=[("phiq_fm",), ("S_bf",)], writes=[yak])
                dps = ps[4] if g < 4 else ps[5]
                dpk = ("psb", 4) if g < 4 else ("psb", 5)
                gg = g % 4
                if not nosd:
                    for k in range(4):
                        h = 4 * g + k
                        mm(dps[32 * k:32 * k + 32, 66 * gg:66 * (gg + 1)],
                           sb["phik_tm"][:, 32 * h:32 * h + 32],
                           sb["vsaug"][:, 66 * h:66 * (h + 1)],
                           start=True, stop=True, tp=(0, 32 * k),
                           reads=[("phik_tm",), ("vsaug",)], writes=[dpk])
                if g == 0 and f"dbg_ya_{t}_{c}" in dbg_t:
                    cp("vector", sb["aff"][:, 0:264], yaps[:, 0:264],
                       reads=[yak], writes=[("aff",)])
                    dbg_store(f"dbg_ya_{t}_{c}", sb["aff"][:, 0:264], [("aff",)])
                cp("vector", sb["ya_sb"][:, 264 * g:264 * (g + 1)], yaps[:, 0:264],
                   reads=[yak], writes=[("ya_sb", g)])

            yakeys = [("ya_sb", g) for g in range(8)]
            YAS = sb["ya_sb"].rearrange("p (h m s) -> p h m s", m=2, s=33)
            EIA = sb["epsinvb"].rearrange("p (m h) -> p h m", h=32)
            tt("vector", sb["dsb2"].rearrange("p (h m) -> p h m", m=2).unsqueeze(3),
               YAS[:, :, :, 32:33], EIA.unsqueeze(3), ALU.add,
               reads=yakeys + [("epsinvb",)], writes=[("dsb2",)])
            recip(sb["rsb2"], sb["dsb2"], reads=[("dsb2",)], writes=[("rsb2",)])
            MXA = sb["mix_tm"].rearrange("p (m h) -> p h m", h=32)
            tt("vector", sb["wsb2"].rearrange("p (h m) -> p h m", m=2),
               sb["rsb2"].rearrange("p (h m) -> p h m", m=2), MXA, ALU.mult,
               reads=[("rsb2",), ("mix_tm",)], writes=[("wsb2",)])
            W3A = sb["wsb2"].rearrange("p (h m) -> p h m", m=2)
            tt("vector", sb["aff"][:].rearrange("p (h d) -> p h d", d=32),
               YAS[:, :, 0, 0:32],
               W3A[:, :, 0:1].to_broadcast([128, 32, 32]), ALU.mult,
               reads=yakeys + [("wsb2",)], writes=[("aff",)])
            tt("vector", sb["z_tm"][:].rearrange("p (h d) -> p h d", d=32),
               YAS[:, :, 1, 0:32],
               W3A[:, :, 1:2].to_broadcast([128, 32, 32]), ALU.mult,
               reads=yakeys + [("wsb2",)], writes=[("z_tm",)])
            tt("vector", sb["vy"][:], sb["aff"][:], sb["z_tm"][:], ALU.add,
               reads=[("aff",), ("z_tm",)], writes=[("vy",)])

            for k in range(4):
                SV = S_all[32 * k:32 * k + 32, :].rearrange(
                    "q (g kk m s) -> q g kk m s", kk=4, m=2, s=33)[:, 0:4, k, :, :]
                DV = ps[4][32 * k:32 * k + 32, 0:264].rearrange(
                    "q (g m s) -> q g m s", m=2, s=33)
                tt("vector", SV, SV, DV, ALU.add, reads=[("S",), ("psb", 4)], writes=[("S",)])
            for k in range(4):
                SV = S_all[32 * k:32 * k + 32, :].rearrange(
                    "q (g kk m s) -> q g kk m s", kk=4, m=2, s=33)[:, 4:8, k, :, :]
                DV = ps[5][32 * k:32 * k + 32, 0:264].rearrange(
                    "q (g m s) -> q g m s", m=2, s=33)
                tt("vector", SV, SV, DV, ALU.add, reads=[("S",), ("psb", 5)], writes=[("S",)])
            SV4 = S_all[:].rearrange("p (h m s) -> p h m s", m=2, s=33)
            BC3 = sb["bCb"].rearrange("p (m h) -> p h m", h=32)
            tt("vector", SV4, SV4, BC3.unsqueeze(3).to_broadcast([128, 32, 2, 33]),
               ALU.mult, reads=[("S",), ("bCb",)], writes=[("S",)])

            dbg_store(f"dbg_y_{t}_{c}", sb["vy"][:], [("vy",)])
            if sub < 6:
                return False
            return True

        def stage_W(c):
            cp("vector", sb["tl2_bf"], sb["vy"][:], reads=[("vy",)], writes=[("tl2",)])
            for j in range(8):
                trans(ps[6][:, j * 128:(j + 1) * 128], sb["tl2_bf"][:, j * 128:(j + 1) * 128],
                      IBF, reads=[("tl2",), ("ident_bf_s",)], writes=[("psb", 6)])
            cp("vector", sb["tl_fm"], ps[6][:], reads=[("psb", 6)], writes=[("tlfm",)])
            for j in range(8):
                for th in range(2):
                    mm(ps[2 + th][:, 0:512], sb["tl_fm"][:, j * 128:(j + 1) * 128],
                       sb["wo_s"][:, j * 1024 + th * 512: j * 1024 + (th + 1) * 512],
                       start=(j == 0), stop=False,
                       reads=[("tlfm",), ("wo_s",)], writes=[("psb", 2 + th)])
            for th in range(2):
                mm(ps[2 + th][:, 0:512], sb["ones_bf_s"][0:1, 0:128],
                   sb["bout_s"][0:1, th * 512:(th + 1) * 512],
                   start=False, stop=True,
                   reads=[("ones_bf_s",), ("bout_s",)], writes=[("psb", 2 + th)])
            stt("vector", sb["z_tm"][:, 0:512], ps[2][:, 0:512], 1.0, sb["a_tm"][:, 0:512],
                ALU.mult, ALU.add, reads=[("psb", 2), ("a_tm",)], writes=[("z_tm",)])
            stt("vector", sb["z_tm"][:, 512:1024], ps[3][:, 0:512], 1.0, sb["a_tm"][:, 512:1024],
                ALU.mult, ALU.add, reads=[("psb", 3), ("a_tm",)], writes=[("z_tm",)])

        def stage_Tl(c):
            xc = x_sb[:, c * H:(c + 1) * H]
            xck = ("x_sb", c)
            rstd2, nb2 = ln_stats(sb["z_tm"][:], [("z_tm",)], 32)
            act(sb["tl2_bf"], sb["z_tm"][:], AF.Identity, reads=[("z_tm",), ("st", 34), ("st", 35)],
                writes=[("tl2",)], bias=nb2, scale=rstd2)
            tt("vector", sb["tl2_bf"], sb["tl2_bf"], sb["lng_s"][:], ALU.mult,
               reads=[("tl2",), ("lng_s",)], writes=[("tl2",)])
            tt("vector", sb["tl2_bf"], sb["tl2_bf"], sb["lnb_s"][:], ALU.add,
               reads=[("tl2",), ("lnb_s",)], writes=[("tl2",)])
            rstd3, nb3 = ln_stats(sb["tl2_bf"], [("tl2",)], 16)
            act(sb["aff"][:], sb["tl2_bf"], AF.Identity, reads=[("tl2",), ("st", 18), ("st", 19)],
                writes=[("aff",)], bias=nb3, scale=rstd3)
            tt("vector", xc, xc, sb["aff"][:], ALU.add, reads=[xck, ("aff",)], writes=[xck])
            dbg_store(f"dbg_x1_{t}_{c}", xc, [xck])
            rstd4, nb4 = ln_stats(xc, [xck], 40)
            act(sb["tl_bf"], xc, AF.Identity, reads=[xck, ("st", 42), ("st", 43)],
                writes=[("tlz",)], bias=nb4, scale=rstd4)

        def stage_Z(c):
            for j in range(8):
                trans(ps[6][:, j * 128:(j + 1) * 128], sb["tl_bf"][:, j * 128:(j + 1) * 128],
                      IBF, reads=[("tlz",), ("ident_bf_s",)], writes=[("psb", 6)])
            cp("vector",
               z1_fm[:].rearrange("p (j cc f) -> p j cc f", cc=8, f=128)[:, :, c, :],
               ps[6][:].rearrange("p (j f) -> p j f", f=128),
               reads=[("psb", 6)], writes=[("z1", c)])
            pass

        stage_A(0)
        cut = False
        for c in range(NCHUNK):
            if not stage_B(c):
                cut = True
                continue
            if c >= 1:
                stage_Z(c - 1)
            stage_W(c)
            if c < NCHUNK - 1:
                stage_A(c + 1)
            stage_Tl(c)
        if not cut:
            stage_Z(NCHUNK - 1)

        P.guard()

        if stage < 2:
            for c in range(NCHUNK):
                dma("sync", OUT[t * T_TILE + c * C: t * T_TILE + (c + 1) * C, :],
                    x_sb[:, c * H:(c + 1) * H], reads=[("x_sb", c)], writes=[("outd", t, c)])
            continue

        for c in range(NCHUNK):
            for j in range(8):
                mm(ps[7][:, 0:16], z1_fm[:, j * 1024 + c * 128: j * 1024 + (c + 1) * 128],
                   sb["wr_s"][:, j * 16:(j + 1) * 16],
                   start=(j == 0), stop=(j == 7), reads=[("z1", c), ("wr_s",)],
                   writes=[("psb", 7)])
            act(sb["e_all"][:, 16 * c:16 * (c + 1)], ps[7][:, 0:16], AF.Exp,
                reads=[("psb", 7)], writes=[("eall", c)], scale=-1.0)

        def router_chain():
            eks = [("eall", c) for c in range(NCHUNK)]
            V = lambda nm: sb[nm]
            ts("vector", V("sc_all"), V("e_all"), 1.0, ALU.add, reads=eks, writes=[("sc_all",)])
            recip(V("scores_a"), V("sc_all"), reads=[("sc_all",)], writes=[("scores_a",)])
            tt("vector", V("sc_all").rearrange("p (c e) -> p c e", e=16),
               V("scores_a").rearrange("p (c e) -> p c e", e=16),
               sb["ebias_s"][:].unsqueeze(1).to_broadcast([128, 8, 16]),
               ALU.add, reads=[("scores_a",), ("ebias_s",)], writes=[("sc_all",)])
            SC = V("sc_all").rearrange("p (c g j) -> p c g j", g=4, j=4)
            GW = V("gsw")
            g32 = lambda o: GW[:, o:o + 32].rearrange("p (c g) -> p c g", g=4)
            tt("vector", g32(0), SC[:, :, :, 0], SC[:, :, :, 1], ALU.max,
               reads=[("sc_all",)], writes=[("gsw", 0)])
            tt("vector", g32(32), SC[:, :, :, 0], SC[:, :, :, 1], ALU.min,
               reads=[("sc_all",)], writes=[("gsw", 1)])
            tt("vector", g32(64), SC[:, :, :, 2], SC[:, :, :, 3], ALU.max,
               reads=[("sc_all",)], writes=[("gsw", 2)])
            tt("vector", g32(96), SC[:, :, :, 2], SC[:, :, :, 3], ALU.min,
               reads=[("sc_all",)], writes=[("gsw", 3)])
            tt("vector", g32(128), g32(0), g32(64), ALU.max,
               reads=[("gsw", 0), ("gsw", 2)], writes=[("gsw", 4)])
            tt("vector", g32(160), g32(0), g32(64), ALU.min,
               reads=[("gsw", 0), ("gsw", 2)], writes=[("gsw", 5)])
            tt("vector", g32(192), g32(32), g32(96), ALU.max,
               reads=[("gsw", 1), ("gsw", 3)], writes=[("gsw", 6)])
            tt("vector", g32(192), g32(160), g32(192), ALU.max,
               reads=[("gsw", 5), ("gsw", 6)], writes=[("gsw", 6)])
            tt("vector", g32(0), g32(128), g32(192), ALU.add,
               reads=[("gsw", 4), ("gsw", 6)], writes=[("gs",)])
            mset("vector", GW[:, 32:64], 0.0, reads=[], writes=[("gm",)])
            GS = g32(0)
            GM = g32(32)
            for _ in range(2):
                tt("vector", V("m8a").rearrange("p (c u) -> p c u", u=2),
                   GS[:, :, 0:2], GS[:, :, 2:4], ALU.max, reads=[("gs",)], writes=[("m8a",)])
                tt("vector", V("m8b")[:, 0:8],
                   V("m8a").rearrange("p (c u) -> p c u", u=2)[:, :, 0],
                   V("m8a").rearrange("p (c u) -> p c u", u=2)[:, :, 1],
                   ALU.max, reads=[("m8a",)], writes=[("m8b",)])
                tt("vector", g32(64), GS,
                   V("m8b")[:, 0:8].unsqueeze(2).to_broadcast([128, 8, 4]),
                   ALU.is_ge, reads=[("gs",), ("m8b",)], writes=[("geq",)])
                stt("vector", g32(96), g32(64), -1e9,
                    sb["iota4_s"][:].unsqueeze(1).to_broadcast([128, 8, 4]),
                    ALU.mult, ALU.add, reads=[("geq",), ("iota4_s",)], writes=[("gcand",)])
                tt("vector", V("m8a").rearrange("p (c u) -> p c u", u=2),
                   g32(96)[:, :, 0:2], g32(96)[:, :, 2:4], ALU.min,
                   reads=[("gcand",)], writes=[("m8a",)])
                tt("vector", V("m8b")[:, 0:8],
                   V("m8a").rearrange("p (c u) -> p c u", u=2)[:, :, 0],
                   V("m8a").rearrange("p (c u) -> p c u", u=2)[:, :, 1],
                   ALU.min, reads=[("m8a",)], writes=[("m8b",)])
                tt("vector", g32(64), g32(96),
                   V("m8b")[:, 0:8].unsqueeze(2).to_broadcast([128, 8, 4]),
                   ALU.is_equal, reads=[("gcand",), ("m8b",)], writes=[("goh",)])
                tt("vector", GM, GM, g32(64), ALU.add, reads=[("gm",), ("goh",)], writes=[("gm",)])
                stt("vector", GS, g32(64), -1e9, GS, ALU.mult, ALU.add,
                    reads=[("goh",), ("gs",)], writes=[("gs",)])
            cp("vector", V("smask").rearrange("p (c g j) -> p c g j", g=4, j=4),
               GM.unsqueeze(3).to_broadcast([128, 8, 4, 4]),
               reads=[("gm",)], writes=[("smask",)])
            tt("vector", V("masked"), V("sc_all"), V("smask"), ALU.mult,
               reads=[("sc_all",), ("smask",)], writes=[("masked",)])
            MK = lambda: V("masked").rearrange("p (c e) -> p c e", e=16)
            for kk in range(4):
                tt("vector", V("ohk").rearrange("p (c e) -> p c e", e=16)[:, :, 0:8],
                   MK()[:, :, 0:8], MK()[:, :, 8:16], ALU.max,
                   reads=[("masked",)], writes=[("tr8",)])
                OH8 = V("ohk").rearrange("p (c e) -> p c e", e=16)
                tt("vector", OH8[:, :, 8:12], OH8[:, :, 0:4], OH8[:, :, 4:8], ALU.max,
                   reads=[("tr8",)], writes=[("tr4",)])
                tt("vector", V("m8a").rearrange("p (c u) -> p c u", u=2),
                   OH8[:, :, 8:10], OH8[:, :, 10:12], ALU.max,
                   reads=[("tr4",)], writes=[("m8a",)])
                tt("vector", V("m8b")[:, 0:8],
                   V("m8a").rearrange("p (c u) -> p c u", u=2)[:, :, 0],
                   V("m8a").rearrange("p (c u) -> p c u", u=2)[:, :, 1],
                   ALU.max, reads=[("m8a",)], writes=[("m8b",)])
                tt("vector", V("ohk"), V("masked"),
                   V("m8b")[:, 0:8].unsqueeze(2).to_broadcast([128, 8, 16]),
                   ALU.is_ge, reads=[("masked",), ("m8b",)], writes=[("ohk",)])
                stt("vector", V("cand").rearrange("p (c e) -> p c e", e=16),
                    V("ohk").rearrange("p (c e) -> p c e", e=16), -1e9,
                    sb["iota16_s"][:].unsqueeze(1).to_broadcast([128, 8, 16]),
                    ALU.mult, ALU.add, reads=[("ohk",), ("iota16_s",)], writes=[("cand",)])
                CD = V("cand").rearrange("p (c e) -> p c e", e=16)
                tt("vector", V("ohk").rearrange("p (c e) -> p c e", e=16)[:, :, 0:8],
                   CD[:, :, 0:8], CD[:, :, 8:16], ALU.min, reads=[("cand",)], writes=[("tr8",)])
                tt("vector", OH8[:, :, 8:12], OH8[:, :, 0:4], OH8[:, :, 4:8], ALU.min,
                   reads=[("tr8",)], writes=[("tr4",)])
                tt("vector", V("m8a").rearrange("p (c u) -> p c u", u=2),
                   OH8[:, :, 8:10], OH8[:, :, 10:12], ALU.min,
                   reads=[("tr4",)], writes=[("m8a",)])
                tt("vector", V("m8b")[:, 0:8],
                   V("m8a").rearrange("p (c u) -> p c u", u=2)[:, :, 0],
                   V("m8a").rearrange("p (c u) -> p c u", u=2)[:, :, 1],
                   ALU.min, reads=[("m8a",)], writes=[("m8b",)])
                tt("vector", V("ohk"), V("cand"),
                   V("m8b")[:, 0:8].unsqueeze(2).to_broadcast([128, 8, 16]),
                   ALU.is_equal, reads=[("cand",), ("m8b",)], writes=[("ohk",)])
                tt("vector", V("cand"), V("scores_a"), V("ohk"), ALU.mult,
                   reads=[("scores_a",), ("ohk",)], writes=[("wkc",)])
                tt("vector", CD[:, :, 0:8], CD[:, :, 0:8], CD[:, :, 8:16], ALU.add,
                   reads=[("wkc",)], writes=[("wkc",)])
                tt("vector", CD[:, :, 0:4], CD[:, :, 0:4], CD[:, :, 4:8], ALU.add,
                   reads=[("wkc",)], writes=[("wkc",)])
                tt("vector", CD[:, :, 0:2], CD[:, :, 0:2], CD[:, :, 2:4], ALU.add,
                   reads=[("wkc",)], writes=[("wkc",)])
                tt("vector", V("wk4")[:, 8 * kk:8 * kk + 8],
                   CD[:, :, 0], CD[:, :, 1], ALU.add,
                   reads=[("wkc",)], writes=[("wk4", kk)])
                stt("vector", V("masked"), V("ohk"), -1e9, V("masked"),
                    ALU.mult, ALU.add, reads=[("ohk",), ("masked",)], writes=[("masked",)])
                dst = V("combA") if kk % 2 == 0 else V("combB")
                srx = V("combB") if kk % 2 == 0 else V("combA")
                if kk == 0:
                    tt("vector", dst.rearrange("p (c e) -> p c e", e=16),
                       V("ohk").rearrange("p (c e) -> p c e", e=16),
                       V("wk4")[:, 0:8].unsqueeze(2).to_broadcast([128, 8, 16]),
                       ALU.mult, reads=[("ohk",), ("wk4", 0)], writes=[("comb2", 0)])
                else:
                    tt("vector", V("cand").rearrange("p (c e) -> p c e", e=16),
                       V("ohk").rearrange("p (c e) -> p c e", e=16),
                       V("wk4")[:, 8 * kk:8 * kk + 8].unsqueeze(2).to_broadcast([128, 8, 16]),
                       ALU.mult, reads=[("ohk",), ("wk4", kk)], writes=[("wkc",)])
                    tt("vector", dst, srx, V("cand"), ALU.add,
                       reads=[("comb2", kk - 1), ("wkc",)], writes=[("comb2", kk)])
            W4 = V("wk4")
            tt("vector", V("m8a")[:, 0:8], W4[:, 0:8], W4[:, 8:16], ALU.add,
               reads=[("wk4", 0), ("wk4", 1)], writes=[("m8a",)])
            tt("vector", V("m8b")[:, 0:8], W4[:, 16:24], W4[:, 24:32], ALU.add,
               reads=[("wk4", 2), ("wk4", 3)], writes=[("m8b",)])
            ts("vector", V("m8a")[:, 0:8], V("m8a")[:, 0:8], 1e-20, ALU.add,
               reads=[("m8a",)], writes=[("m8a",)])
            tt("vector", V("m8a")[:, 0:8], V("m8a")[:, 0:8], V("m8b")[:, 0:8], ALU.add,
               reads=[("m8a",), ("m8b",)], writes=[("m8a",)])
            recip(V("m8b")[:, 0:8], V("m8a")[:, 0:8], reads=[("m8a",)], writes=[("m8b",)])
            ts("vector", V("m8b")[:, 0:8], V("m8b")[:, 0:8], 2.5, ALU.mult,
               reads=[("m8b",)], writes=[("m8b",)])
            tt("vector", V("comb_all").rearrange("p (c e) -> p c e", e=16),
               V("combB").rearrange("p (c e) -> p c e", e=16),
               V("m8b")[:, 0:8].unsqueeze(2).to_broadcast([128, 8, 16]),
               ALU.mult, reads=[("comb2", 3), ("m8b",)], writes=[("comball",)])

        def comb_transposes():
            for c in range(NCHUNK):
                trans(ps[7][0:16, 0:128], sb["comb_all"][:, 16 * c:16 * (c + 1)],
                      IF, reads=[("comball",), ("ident_f32_s",)], writes=[("psb", 7)])
                cp("vector", sb["comb_fm"][:, 128 * c:128 * (c + 1)], ps[7][0:16, 0:128],
                   reads=[("psb", 7)], writes=[("combfm", c)])

        if stage < 3:
            router_chain()
            comb_transposes()
            for c in range(NCHUNK):
                dma("sync", OUT[t * T_TILE + c * C: t * T_TILE + (c + 1) * C, :],
                    x_sb[:, c * H:(c + 1) * H], reads=[("x_sb", c)], writes=[("outd", t, c)])
            P.guard()
            continue

        for e in range(NE):
            gwn, uwn = f"gww{e % 2}", f"uww{e % 2}"
            dwn = f"dww{e % EGRP}"
            dma("gpsimd", sb[gwn].rearrange("p (j f) -> p j f", f=512),
                dram["gwT"][e].rearrange("(j p) f -> p j f", p=128), reads=[], writes=[(gwn,)])
            dma("gpsimd", sb[uwn].rearrange("p (j f) -> p j f", f=512),
                dram["uwT"][e].rearrange("(j p) f -> p j f", p=128), reads=[], writes=[(uwn,)])
            dma("gpsimd", sb[dwn].rearrange("p (j f) -> p j f", f=1024),
                dram["dwT"][e].rearrange("(j p) f -> p j f", p=128), reads=[], writes=[(dwn,)])
            hbuf = sb[f"h{e % EGRP}"]
            hk = ("h", e % EGRP)
            for ic in range(4):
                for th in range(2):
                    for j in range(8):
                        mm(ps[th][:, 0:512],
                           sb[gwn][:, j * 512 + ic * 128: j * 512 + (ic + 1) * 128],
                           z1_fm[:, j * 1024 + th * 512: j * 1024 + (th + 1) * 512],
                           start=(j == 0), stop=(j == 7),
                           reads=[(gwn,)] + [("z1", cc) for cc in range(th * 4, th * 4 + 4)],
                           writes=[("psb", th)])
                    for j in range(8):
                        mm(ps[2 + th][:, 0:512],
                           sb[uwn][:, j * 512 + ic * 128: j * 512 + (ic + 1) * 128],
                           z1_fm[:, j * 1024 + th * 512: j * 1024 + (th + 1) * 512],
                           start=(j == 0), stop=(j == 7),
                           reads=[(uwn,)] + [("z1", cc) for cc in range(th * 4, th * 4 + 4)],
                           writes=[("psb", 2 + th)])
                for th in range(2):
                    act(sb["s_sb"][:, th * 512:(th + 1) * 512], ps[th][:, 0:512], AF.Silu,
                        reads=[("psb", th)], writes=[("s_sb", th)])
                for th in range(2):
                    tt("vector", hbuf[:, ic * 1024 + th * 512: ic * 1024 + (th + 1) * 512],
                       sb["s_sb"][:, th * 512:(th + 1) * 512], ps[2 + th][:, 0:512],
                       ALU.mult, reads=[("s_sb", th), ("psb", 2 + th)], writes=[hk])
            if e == 0:
                router_chain()
                comb_transposes()
            if e < 16:
                dma("sync", sb["crow"][0:1, :], sb["comb_fm"][e:e + 1, :],
                    reads=[("combfm", cc) for cc in range(8)], writes=[("crow",)])
                for th in range(2):
                    mm(ps[7][:, 0:512], sb["ones_bf_s"][0:1, 0:128],
                       sb["crow"][0:1, th * 512:(th + 1) * 512],
                       start=True, stop=True,
                       reads=[("crow",), ("ones_bf_s",)],
                       writes=[("psb", 7)])
                    cp("vector", sb["bcomb"][:, th * 512:(th + 1) * 512], ps[7][:, 0:512],
                       reads=[("psb", 7)], writes=[("bcomb", th)])
                tt("vector", hbuf[:].rearrange("p (ic f) -> p ic f", f=1024),
                   hbuf[:].rearrange("p (ic f) -> p ic f", f=1024),
                   sb["bcomb"][:].unsqueeze(1).to_broadcast([128, 4, 1024]),
                   ALU.mult, reads=[hk, ("bcomb", 0), ("bcomb", 1)], writes=[hk])
            if e % EGRP == EGRP - 1:
                grp0 = e - EGRP + 1
                first = grp0 == 0
                for fc in range(8):
                    for th in range(2):
                        pd = ps[4] if (fc * 2 + th) % 2 == 0 else ps[5]
                        pdk = ("psb", 4 + (fc * 2 + th) % 2)
                        nmm = 0
                        for ee in range(grp0, grp0 + EGRP):
                            hb = sb[f"h{ee % EGRP}"]
                            dwe = f"dww{ee % EGRP}"
                            for ic in range(4):
                                nmm += 1
                                mm(pd[:, 0:512],
                                   sb[dwe][:, ic * 1024 + fc * 128: ic * 1024 + (fc + 1) * 128],
                                   hb[:, ic * 1024 + th * 512: ic * 1024 + th * 512 + 512],
                                   start=(nmm == 1), stop=(nmm == EGRP * 4),
                                   reads=[(dwe,), ("h", ee % EGRP)], writes=[pdk])
                        dst = sb["routed"][:, fc * 1024 + th * 512: fc * 1024 + (th + 1) * 512]
                        if first:
                            cp("vector", dst, pd[:, 0:512], reads=[pdk],
                               writes=[("routed", fc, th)])
                        else:
                            stt("vector", dst, pd[:, 0:512], 1.0, dst, ALU.mult, ALU.add,
                                reads=[pdk, ("routed", fc, th)], writes=[("routed", fc, th)])

        if stage < 4:
            for c in range(NCHUNK):
                dma("sync", OUT[t * T_TILE + c * C: t * T_TILE + (c + 1) * C, :],
                    x_sb[:, c * H:(c + 1) * H], reads=[("x_sb", c)], writes=[("outd", t, c)])
            P.guard()
            continue

        for c in range(NCHUNK):
            xc = x_sb[:, c * H:(c + 1) * H]
            xck = ("x_sb", c)
            for half in range(2):
                for jj in range(4):
                    j = half * 4 + jj
                    trans(ps[7][:, jj * 128:(jj + 1) * 128],
                          sb["routed"][:, j * 1024 + c * 128: j * 1024 + (c + 1) * 128],
                          IF, reads=[("routed", j, c // 4), ("ident_f32_s",)],
                          writes=[("psb", 7)])
                cp("vector", sb["r_tm"][:, half * 512:(half + 1) * 512], ps[7][:, 0:512],
                   reads=[("psb", 7)], writes=[("r_tm", half)])
            rstd5, nb5 = ln_stats(sb["r_tm"], [("r_tm", 0), ("r_tm", 1)], 44)
            act(sb["r_tm"], sb["r_tm"], AF.Identity,
                reads=[("r_tm", 0), ("r_tm", 1), ("st", 46), ("st", 47)],
                writes=[("r_tm", 0), ("r_tm", 1)], bias=nb5, scale=rstd5)
            tt("vector", xc, xc, sb["r_tm"], ALU.add,
               reads=[xck, ("r_tm", 0), ("r_tm", 1)], writes=[xck])
            dma("sync", OUT[t * T_TILE + c * C: t * T_TILE + (c + 1) * C, :], xc,
                reads=[xck], writes=[("outd", t, c)])

        P.guard()

    P.finalize()
    return nc, P, dbg_t


def _run(nc, P, in_maps, core_ids):
    with (
        nc.semaphore("sem_t") as s_t,
        nc.semaphore("sem_v") as s_v,
        nc.semaphore("sem_s") as s_s,
        nc.semaphore("sem_y") as s_y,
        nc.semaphore("sem_g") as s_g,
        nc.Block() as blk,
    ):
        sems = {"tensor": s_t, "vector": s_v, "scalar": s_s, "sync": s_y, "gpsimd": s_g}

        def runner(eng):
            def go(E):
                for op in P.ops[eng]:
                    for (e2, tv) in op[4]:
                        E.wait_ge(sems[e2], tv)
                    hnd = op[0](E)
                    if op[3]:
                        hnd.then_inc(sems[eng], 16 if op[2] else 1)
                if eng == "sync":
                    for e2 in ENGINES:
                        tot = P.total(e2)
                        if tot:
                            E.wait_ge(sems[e2], tot)
            return go

        blk.tensor(runner("tensor"))
        blk.vector(runner("vector"))
        blk.scalar(runner("scalar"))
        blk.sync(runner("sync"))
        blk.gpsimd(runner("gpsimd"))

    return run_bass_kernel_spmd(nc, in_maps, core_ids=core_ids)


def kernel(**inputs):
    host = _prep_host(inputs)
    x = np.ascontiguousarray(np.asarray(inputs["L0_attn_x"], dtype=np.float32))
    nc, P, _ = build_nc(n_tiles=4)
    in_maps = []
    for core in range(8):
        m = dict(host)
        m["x"] = x[2 * core:2 * core + 2].reshape(4096, H)
        in_maps.append(m)
    res = _run(nc, P, in_maps, core_ids=list(range(8)))
    out = np.empty((16, 2048, H), np.float32)
    for core in range(8):
        out[2 * core:2 * core + 2] = res.results[core]["out"].reshape(2, 2048, H)
    return out


# version 12
# speedup vs baseline: 1.0466x; 1.0466x over previous
import numpy as np
import ml_dtypes
import concourse.bass as bass
import concourse.mybir as mybir
from concourse.bass_utils import run_bass_kernel_spmd

F32 = mybir.dt.float32
BF16 = mybir.dt.bfloat16
AF = mybir.ActivationFunctionType
ALU = mybir.AluOpType
AX = mybir.AxisListType

T_TILE = 1024
NCHUNK = 8
C = 128
H = 1024
NH = 32
NE = 18
EGRP = 3
LN_EPS = 1e-5
ATT_EPS = 1e-6
BMIN, BMAX = 0.85, 0.9995

ENGINES = ("tensor", "vector", "scalar", "sync", "gpsimd")

ARENA_TAGS = {
    "wq_s", "wk_s", "wv_s", "wo_s", "wbm_s", "a_tm", "a_bf", "a_fm", "qke", "rr",
    "tmp", "phiq_tm", "phik_tm", "phiq_fm", "phik_fm", "vy", "bm_sb", "ebx",
    "rbeta", "invb_fm", "mix_fm", "mw", "invb_tm", "mix_tm", "epsinvb", "bC_col",
    "dsb2", "rsb2", "wsb2",
    "bC_row", "bCb", "vsaug", "ptb", "z_tm", "aff", "tl2", "tlz", "tlfm",
    "routed", "h", "gww0", "gww1", "uww0", "uww1", "dww0", "dww1", "dww2",
    "s_sb", "bcomb", "sc_all", "scores_a", "gsw", "m8a", "m8b",
    "masked", "smask", "ohk", "cand", "wk4",
    "wks", "comb2", "comb", "combfm", "crow", "r_tm", "ya_sb",
}


class Prog:

    def __init__(self):
        self.ops = {e: [] for e in ENGINES}
        self.last_w = {}
        self.readers = {}

    def emit(self, eng, fn, reads=(), writes=(), dma=False, force_mark=False):
        reads = list(reads)
        if any(k[0] in ARENA_TAGS for k in list(reads) + list(writes)):
            reads.append(("AR",))
        idx = len(self.ops[eng])
        deps = set()
        for k in reads:
            w = self.last_w.get(k)
            if w is not None:
                deps.add(w)
            self.readers.setdefault(k, []).append((eng, idx))
        for k in writes:
            w = self.last_w.get(k)
            if w is not None:
                deps.add(w)
            for r in self.readers.get(k, ()):
                deps.add(r)
            self.readers[k] = []
            self.last_w[k] = (eng, idx)
        final = []
        sdeps = []
        for (e2, i2) in deps:
            if (e2, i2) == (eng, idx):
                continue
            if e2 == eng and not self.ops[e2][i2][2]:
                if eng == "vector" and idx - i2 < 8:
                    final.append((e2, i2))
                else:
                    sdeps.append(i2)
                continue
            final.append((e2, i2))
        self.ops[eng].append([fn, final, dma, force_mark, None, sdeps])

    def guard(self):
        self.emit("vector", lambda E: E.engine_nop(), reads=[], writes=[("AR",)])

    def _insert_vector_nops(self):
        old = self.ops["vector"]
        new = []
        newpos = []
        self.n_nops = 0
        for i, op in enumerate(old):
            dmax = max(op[5], default=-1)
            if dmax >= 0 and newpos[dmax] == len(new) - 1:
                new.append([lambda E: E.tensor_copy(out=self._jnk[:, 1:2],
                                                    in_=self._jnk[:, 0:1]),
                            [], False, False, None, []])
                self.n_nops += 1
            newpos.append(len(new))
            new.append(op)
        self.ops["vector"] = new
        for e in ENGINES:
            for op in self.ops[e]:
                op[1] = [(e2, newpos[i2] if e2 == "vector" else i2) for (e2, i2) in op[1]]

    def finalize(self):
        for e in ENGINES:
            for op in self.ops[e]:
                for (e2, i2) in op[1]:
                    self.ops[e2][i2][3] = True
        cum = {}
        for e in ENGINES:
            cum[e] = []
            v = 0
            for op in self.ops[e]:
                if op[3]:
                    v += 16 if op[2] else 1
                cum[e].append(v)
        know = {e: {e2: 0 for e2 in ENGINES} for e in ENGINES}
        nwaits = 0
        for e in ENGINES:
            for op in self.ops[e]:
                waits = {}
                for (e2, i2) in op[1]:
                    tv = cum[e2][i2]
                    if know[e][e2] < tv:
                        waits[e2] = max(waits.get(e2, 0), tv)
                for e2, tv in waits.items():
                    know[e][e2] = tv
                op[4] = sorted(waits.items())
                nwaits += len(op[4])
        self.cum = cum
        self.nwaits = nwaits

    def total(self, e):
        return self.cum[e][-1] if self.ops[e] else 0


def _prep_host(inputs):
    f32 = np.float32
    bf16 = ml_dtypes.bfloat16
    d = {}
    d["wqT"] = np.ascontiguousarray(inputs["L0_attn_Wq"].T, dtype=f32)
    d["wkT"] = np.ascontiguousarray(inputs["L0_attn_Wk"].T, dtype=f32)
    d["wvT"] = np.ascontiguousarray(inputs["L0_attn_Wv"].T, dtype=f32)
    d["woT"] = np.ascontiguousarray(inputs["L0_attn_Wout"].T, dtype=f32)
    perm = np.array([(ch % 32) * 2 + (ch // 32) for ch in range(64)])
    wb = np.asarray(inputs["L0_attn_Wbeta"])[perm]
    wm = np.asarray(inputs["L0_attn_Wmix"])[perm]
    d["wbmT"] = np.ascontiguousarray(np.concatenate([wb, wm], 0).T, dtype=f32)
    d["bbm_col"] = np.concatenate([np.asarray(inputs["L0_attn_bbeta"])[perm],
                                   np.asarray(inputs["L0_attn_bmix"])[perm]]).reshape(128, 1).astype(f32)
    d["bout_row"] = np.asarray(inputs["L0_attn_bout"]).reshape(1, 1024).astype(bf16)
    d["lng_b"] = np.ascontiguousarray(np.broadcast_to(inputs["L0_attn_ln_g"], (128, 1024)), dtype=bf16)
    d["lnb_b"] = np.ascontiguousarray(np.broadcast_to(inputs["L0_attn_ln_b"], (128, 1024)), dtype=bf16)
    d["wrT"] = np.ascontiguousarray(inputs["L0_ffn_router_weight"].T, dtype=f32)
    d["ebias_b"] = np.ascontiguousarray(np.broadcast_to(inputs["L0_ffn_e_bias"], (128, 16)), dtype=f32)
    gw, uw, dw = (np.asarray(inputs["L0_ffn_gate_w"]), np.asarray(inputs["L0_ffn_up_w"]),
                  np.asarray(inputs["L0_ffn_down_w"]))
    shgT = np.asarray(inputs["L0_ffn_sh_gate_w"]).T
    shuT = np.asarray(inputs["L0_ffn_sh_up_w"]).T
    shdT = np.asarray(inputs["L0_ffn_sh_down_w"]).T
    gts = [gw[e].T for e in range(16)] + [shgT[:, 0:512], shgT[:, 512:1024]]
    uts = [uw[e].T for e in range(16)] + [shuT[:, 0:512], shuT[:, 512:1024]]
    dts = [dw[e].T for e in range(16)] + [shdT[0:512], shdT[512:1024]]
    d["gwT"] = np.ascontiguousarray(np.stack(gts), dtype=f32)
    d["uwT"] = np.ascontiguousarray(np.stack(uts), dtype=f32)
    d["dwT"] = np.ascontiguousarray(np.stack(dts), dtype=f32)
    half = 16
    inv_freq = 1.0 / 10000.0 ** (np.arange(half, dtype=np.float64) / half)
    freqs = np.arange(2048, dtype=np.float64)[:, None] * inv_freq
    d["cosrep"] = np.tile(np.cos(freqs), (1, 32)).astype(bf16)
    d["sinrep"] = np.tile(np.sin(freqs), (1, 32)).astype(bf16)
    d["triu"] = np.triu(np.ones((128, 128), f32))
    d["ident_bf"] = np.eye(128, dtype=bf16)
    d["ident_f32"] = np.eye(128, dtype=f32)
    d["iota16"] = np.broadcast_to(np.arange(16, dtype=f32), (128, 16)).copy()
    d["iota4"] = np.broadcast_to(np.arange(4, dtype=f32), (128, 4)).copy()
    d["ones_f32"] = np.ones((1, 128), f32)
    d["ones_bf"] = np.ones((1, 128), bf16)
    sd = np.zeros((128, 32), f32)
    for hh in range(32):
        sd[64 + hh, hh] = 1.0
        sd[96 + hh, hh] = -1.0
    d["seldiff"] = sd
    return d


def build_nc(n_tiles=4, dbg=(), stage=4, sub=9, nosd=False):
    nc = bass.Bass("TRN2", target_bir_lowering=False, detect_race_conditions=False)
    T_core = n_tiles * T_TILE

    X = nc.dram_tensor("x", [T_core, H], F32, kind="ExternalInput").ap()
    OUT = nc.dram_tensor("out", [T_core, H], F32, kind="ExternalOutput").ap()
    dram = {}
    for nm, shp, dt in [
        ("wqT", [H, H], F32), ("wkT", [H, H], F32), ("wvT", [H, H], F32),
        ("woT", [H, H], F32), ("wbmT", [H, 128], F32), ("bbm_col", [128, 1], F32),
        ("bout_row", [1, H], BF16), ("lng_b", [128, H], BF16), ("lnb_b", [128, H], BF16),
        ("wrT", [H, 16], F32), ("ebias_b", [128, 16], F32),
        ("gwT", [NE, H, 512], F32), ("uwT", [NE, H, 512], F32), ("dwT", [NE, 512, H], F32),
        ("cosrep", [2048, 512], BF16), ("sinrep", [2048, 512], BF16),
        ("triu", [128, 128], F32), ("ident_bf", [128, 128], BF16),
        ("ident_f32", [128, 128], F32), ("iota16", [128, 16], F32),
        ("iota4", [128, 4], F32), ("ones_f32", [1, 128], F32), ("ones_bf", [1, 128], BF16),
        ("seldiff", [128, 32], F32),
    ]:
        dram[nm] = nc.dram_tensor(nm, shp, dt, kind="ExternalInput").ap()
    dbg_t = {nm: nc.dram_tensor(nm, shp, dt, kind="ExternalOutput").ap()
             for nm, shp, dt in dbg}

    sb = {}

    def S(name, shape, dt):
        sb[name] = nc.alloc_sbuf_tensor(name, shape, dt)
        return sb[name]

    x_sb = S("x_sb", [128, NCHUNK * H], F32)
    S_all = S("S_state", [128, NH * 66], F32)
    S("S_bf", [128, NH * 66], BF16)
    z1_fm = S("z1_fm", [128, 8 * T_TILE], BF16)
    S("triu_s", [128, 128], F32)
    S("ident_f32_s", [128, 128], F32)
    S("ident_bf_s", [128, 128], BF16)
    S("lng_s", [128, H], BF16)
    S("lnb_s", [128, H], BF16)
    S("iota16_s", [128, 16], F32)
    S("iota4_s", [128, 4], F32)
    S("ebias_s", [128, 16], F32)
    S("bbm_s", [128, 1], F32)
    S("ones_f32_s", [1, 128], F32)
    S("seldiff_s", [128, 32], F32)
    S("ones_bf_s", [1, 128], BF16)
    S("bout_s", [1, H], BF16)
    S("wr_s", [128, 8 * 16], BF16)
    S("one64", [64, 128], F32)
    S("csl0", [128, 1024], BF16)
    S("csl1", [128, 1024], BF16)
    S("stat", [128, 64], F32)
    S("e_all", [128, 128], F32)
    sb["e_all"] = sb["e_all"][:]
    S("scr", [128, 1024], BF16)

    AR = S("arena", [128, 66688], BF16)

    def carve(cur, name, n_elems, dt, parts=128):
        off = cur // 2
        if dt == BF16:
            ap = AR[0:parts, off:off + n_elems]
            nxt = cur + 2 * n_elems
        else:
            ap = AR[0:parts, off:off + 2 * n_elems].bitcast(F32)
            nxt = cur + 4 * n_elems
        assert nxt % 32 == 0 or True
        sb[name] = ap
        return nxt

    cur = 0
    for nm, n, dt in [
        ("wq_s", 8192, BF16), ("wk_s", 8192, BF16), ("wv_s", 8192, BF16),
        ("wo_s", 8192, BF16), ("wbm_s", 1024, BF16),
        ("a_tm", 1024, F32), ("a_bf", 1024, BF16), ("a_fm", 1024, BF16),
        ("qke", 1024, BF16), ("rr", 1024, BF16), ("tmp_bf", 1024, BF16),
        ("phiq_tm", 1024, BF16), ("phik_tm", 1024, BF16),
        ("phiq_fm", 1024, BF16), ("phik_fm", 1024, BF16),
        ("vy", 1024, F32), ("bm_sb", 128, F32),
        ("ptb0", 512, BF16), ("ptb1", 512, BF16),
        ("vsaug", 2112, BF16),
        ("invb_tm", 64, F32), ("mix_tm", 64, F32), ("epsinvb", 64, F32),
        ("bC_row", 64, F32), ("bCb", 64, F32),
        ("dsb2", 64, F32), ("rsb2", 64, F32), ("wsb2", 64, F32),
        ("z_tm", 1024, F32), ("aff", 1024, F32),
        ("tl2_bf", 1024, BF16), ("tl_bf", 1024, BF16), ("tl_fm", 1024, BF16),
        ("ya_sb", 2112, F32),
    ]:
        cur = carve(cur, nm, n, dt)
    for nm, n in [("ebx", 128), ("rbeta", 128), ("invb_fm", 128),
                  ("mix_fm", 128), ("mw", 128)]:
        cur = carve(cur, nm, n, F32, parts=64)
    cur = carve(cur, "bC_col", 8, F32, parts=64)
    attn_end = cur

    cur = 0
    for nm, n, dt in [
        ("routed", 8192, F32),
        ("h0", 4096, BF16), ("h1", 4096, BF16), ("h2", 4096, BF16),
        ("gww0", 4096, BF16), ("gww1", 4096, BF16),
        ("uww0", 4096, BF16), ("uww1", 4096, BF16),
        ("dww0", 4096, BF16), ("dww1", 4096, BF16), ("dww2", 4096, BF16),
        ("s_sb", 1024, BF16), ("bcomb", 1024, BF16),
        ("sc_all", 128, F32), ("scores_a", 128, F32),
        ("gsw", 256, F32), ("m8a", 16, F32), ("m8b", 16, F32),
        ("masked", 128, F32), ("smask", 128, F32), ("ohk", 128, F32), ("cand", 128, F32),
        ("wk4", 32, F32), ("combA", 128, F32), ("combB", 128, F32),
        ("comb_all", 128, F32), ("r_tm", 1024, F32),
    ]:
        cur = carve(cur, nm, n, dt)
    cur = carve(cur, "comb_fm", 1024, BF16, parts=16)
    cur = carve(cur, "crow", 1024, BF16, parts=1)
    moe_end = cur
    assert max(attn_end, moe_end) <= 2 * 66688, (attn_end, moe_end)

    ps = {i: nc.alloc_psum_tensor(f"ps{i}", [128, 512], F32) for i in range(6)}
    ps[6] = nc.alloc_psum_tensor("ps6", [128, 1024], BF16)
    ps[7] = nc.alloc_psum_tensor("ps7", [128, 512], F32)

    P = Prog()
    P._jnk = S("jnk", [128, 2], F32)

    def dma(eng, out, in_, reads, writes):
        P.emit(eng, lambda E, o=out, i=in_: E.dma_start(out=o, in_=i),
               reads=reads, writes=writes, dma=True, force_mark=True)

    def mm(out, lhsT, rhs, start, stop, reads, writes, tp=None):
        P.emit("tensor",
               lambda E, o=out, l=lhsT, r=rhs, st=start, sp=stop, t=tp:
               E.matmul(o, l, r, start=st, stop=sp, tile_position=t),
               reads=reads, writes=writes)

    def trans(out, in_, ident, reads, writes):
        P.emit("tensor", lambda E, o=out, i=in_, d=ident: E.transpose(o, i, d),
               reads=reads, writes=writes)

    def act(out, in_, func, reads, writes, bias=0.0, scale=1.0, accum_out=None):
        P.emit("scalar", lambda E, o=out, i=in_, f=func, b=bias, s=scale, a=accum_out:
               E.activation(out=o, in_=i, func=f, bias=b, scale=s, accum_out=a),
               reads=reads, writes=writes)

    def tt(eng, out, in0, in1, op, reads, writes):
        P.emit(eng, lambda E, o=out, a=in0, b=in1, p=op:
               E.tensor_tensor(out=o, in0=a, in1=b, op=p), reads=reads, writes=writes)

    def ts(eng, out, in0, s1, op0, reads, writes, s2=None, op1=ALU.bypass):
        P.emit(eng, lambda E, o=out, a=in0, x=s1, y=s2, p0=op0, p1=op1:
               E.tensor_scalar(out=o, in0=a, scalar1=x, scalar2=y, op0=p0, op1=p1),
               reads=reads, writes=writes)

    def stt(eng, out, in0, scalar, in1, op0, op1, reads, writes):
        P.emit(eng, lambda E, o=out, a=in0, s=scalar, b=in1, p0=op0, p1=op1:
               E.scalar_tensor_tensor(out=o, in0=a, scalar=s, in1=b, op0=p0, op1=p1),
               reads=reads, writes=writes)

    def cp(eng, out, in_, reads, writes):
        if eng == "scalar":
            P.emit(eng, lambda E, o=out, i=in_: E.copy(out=o, in_=i),
                   reads=reads, writes=writes)
        else:
            P.emit(eng, lambda E, o=out, i=in_: E.tensor_copy(out=o, in_=i),
                   reads=reads, writes=writes)

    def red(out, in_, op, reads, writes):
        P.emit("vector", lambda E, o=out, i=in_, p=op: E.tensor_reduce(o, i, AX.X, p),
               reads=reads, writes=writes)

    def recip(out, in_, reads, writes):
        P.emit("vector", lambda E, o=out, i=in_: E.reciprocal(out=o, in_=i),
               reads=reads, writes=writes)

    def mset(eng, ap_, val, reads, writes):
        P.emit(eng, lambda E, a=ap_, v=val: E.memset(a, v), reads=reads, writes=writes)

    def scan(out, d0, d1, init, op0, op1, reads, writes):
        P.emit("vector", lambda E, o=out, a=d0, b=d1, i=init, p0=op0, p1=op1:
               E.tensor_tensor_scan(out=o, data0=a, data1=b, initial=i, op0=p0, op1=p1),
               reads=reads, writes=writes)

    def dbg_store(name, src_ap, reads):
        if name in dbg_t:
            dma("sync", dbg_t[name], src_ap, reads=reads, writes=[("dbg", name)])

    st = sb["stat"]

    def ln_stats(src_ap, src_keys, col, n=1024):
        act(sb["scr"][:, 0:n], src_ap, AF.Identity, reads=src_keys,
            writes=[("scr",), ("st", col)], accum_out=st[:, col:col + 1])
        act(sb["scr"][:, 0:n], src_ap, AF.Square, reads=src_keys,
            writes=[("scr",), ("st", col + 1)], accum_out=st[:, col + 1:col + 2])
        ts("vector", st[:, col:col + 1], st[:, col:col + 1], 1.0 / n, ALU.mult,
           reads=[("st", col)], writes=[("st", col)])
        ts("vector", st[:, col + 1:col + 2], st[:, col + 1:col + 2], 1.0 / n, ALU.mult,
           reads=[("st", col + 1)], writes=[("st", col + 1)])
        stt("vector", st[:, col + 2:col + 3], st[:, col:col + 1], st[:, col:col + 1],
            st[:, col + 1:col + 2], ALU.mult, ALU.subtract,
            reads=[("st", col), ("st", col + 1)], writes=[("st", col + 2)])
        ts("vector", st[:, col + 2:col + 3], st[:, col + 2:col + 3], -1.0, ALU.mult,
           reads=[("st", col + 2)], writes=[("st", col + 2)], s2=LN_EPS, op1=ALU.add)
        act(st[:, col + 3:col + 4], st[:, col + 2:col + 3], AF.Ln,
            reads=[("st", col + 2)], writes=[("st", col + 3)])
        act(st[:, col + 2:col + 3], st[:, col + 3:col + 4], AF.Exp,
            reads=[("st", col + 3)], writes=[("st", col + 2)], scale=-0.5)
        stt("vector", st[:, col + 3:col + 4], st[:, col:col + 1], -1.0,
            st[:, col + 2:col + 3], ALU.mult, ALU.mult,
            reads=[("st", col), ("st", col + 2)], writes=[("st", col + 3)])
        return st[:, col + 2:col + 3], st[:, col + 3:col + 4]

    for nm, tgt in [("triu", "triu_s"), ("ident_f32", "ident_f32_s"), ("lng_b", "lng_s"),
                    ("lnb_b", "lnb_s"), ("ident_bf", "ident_bf_s"), ("iota16", "iota16_s"),
                    ("iota4", "iota4_s"), ("ebias_b", "ebias_s"), ("bbm_col", "bbm_s"),
                    ("ones_f32", "ones_f32_s"), ("ones_bf", "ones_bf_s"), ("bout_row", "bout_s"),
                    ("seldiff", "seldiff_s")]:
        dma("sync", sb[tgt][:], dram[nm], reads=[], writes=[(tgt,)])
    dma("gpsimd", sb["wr_s"][:].rearrange("p (j f) -> p j f", f=16),
        dram["wrT"].rearrange("(j p) f -> p j f", p=128), reads=[], writes=[("wr_s",)])
    mset("vector", sb["one64"][:], 1.0, reads=[], writes=[("one64",)])

    IBF = sb["ident_bf_s"][:]
    IF = sb["ident_f32_s"][:]
    x_keys = [("x_sb", c) for c in range(NCHUNK)]

    for t in range(n_tiles):
        seq_pos0 = (t % 2) * T_TILE

        for wnm, wsn in [("wqT", "wq_s"), ("wkT", "wk_s"), ("wvT", "wv_s"), ("woT", "wo_s")]:
            dma("gpsimd", sb[wsn].rearrange("p (j f) -> p j f", f=1024),
                dram[wnm].rearrange("(j p) f -> p j f", p=128), reads=[], writes=[(wsn,)])
        dma("gpsimd", sb["wbm_s"].rearrange("p (j f) -> p j f", f=128),
            dram["wbmT"].rearrange("(j p) f -> p j f", p=128), reads=[], writes=[("wbm_s",)])

        dma("sync", x_sb[:].rearrange("p (c f) -> p c f", f=H),
            X[t * T_TILE:(t + 1) * T_TILE, :].rearrange("(c p) f -> p c f", p=128),
            reads=[], writes=x_keys)

        dbg_store(f"dbg_xin_{t}", x_sb[:, 0:1024], [("x_sb", 0)])
        if t % 2 == 0:
            mset("vector", S_all[:], 0.0, reads=[], writes=[("S",)])

        def stage_A(c):
            xc = x_sb[:, c * H:(c + 1) * H]
            xck = ("x_sb", c)
            pos0 = seq_pos0 + c * C
            csl = sb["csl0"] if c % 2 == 0 else sb["csl1"]
            ck = "csl0" if c % 2 == 0 else "csl1"
            dma("sync", csl[:, 0:512], dram["cosrep"][pos0:pos0 + 128, :],
                reads=[], writes=[(ck, 0)])
            dma("sync", csl[:, 512:1024], dram["sinrep"][pos0:pos0 + 128, :],
                reads=[], writes=[(ck, 1)])
            rstd, nbias = ln_stats(xc, [xck], 0)
            dbg_store(f"dbg_st_{t}_{c}", st[:, 0:4], [("st", 0), ("st", 1), ("st", 2), ("st", 3)])
            act(sb["a_tm"][:], xc, AF.Identity, reads=[xck, ("st", 2), ("st", 3)],
                writes=[("a_tm",)], bias=nbias, scale=rstd)
            cp("vector", sb["a_bf"][:], sb["a_tm"][:], reads=[("a_tm",)], writes=[("a_bf",)])
            for j in range(8):
                trans(ps[6][:, j * 128:(j + 1) * 128], sb["a_bf"][:, j * 128:(j + 1) * 128],
                      IBF, reads=[("a_bf",), ("ident_bf_s",)], writes=[("psb", 6)])
            cp("scalar", sb["a_fm"][:], ps[6][:], reads=[("psb", 6)], writes=[("a_fm",)])
            dbg_store(f"dbg_a_{t}_{c}", sb["a_tm"][:], [("a_tm",)])

        def stage_B(c):
            xc = x_sb[:, c * H:(c + 1) * H]
            xck = ("x_sb", c)
            csl = sb["csl0"] if c % 2 == 0 else sb["csl1"]
            ck = "csl0" if c % 2 == 0 else "csl1"
            COS = csl[:, 0:512].rearrange("p (h d) -> p h d", d=16)
            SIN = csl[:, 512:1024].rearrange("p (h d) -> p h d", d=16)
            for (wsn, b0) in [("wq_s", 0), ("wk_s", 2), ("wv_s", 4)]:
                for th in range(2):
                    for j in range(8):
                        mm(ps[b0 + th][:, 0:512],
                           sb["a_fm"][:, j * 128:(j + 1) * 128],
                           sb[wsn][:, j * 1024 + th * 512: j * 1024 + (th + 1) * 512],
                           start=(j == 0), stop=(j == 7),
                           reads=[("a_fm",), (wsn,)], writes=[("psb", b0 + th)])
            for j in range(8):
                mm(ps[7][:, 0:128], sb["wbm_s"][:, j * 128:(j + 1) * 128],
                   sb["a_fm"][:, j * 128:(j + 1) * 128],
                   start=(j == 0), stop=(j == 7),
                   reads=[("a_fm",), ("wbm_s",)], writes=[("psb", 7)])
            act(sb["bm_sb"][:], ps[7][:, 0:128], AF.Identity,
                reads=[("psb", 7), ("bbm_s",)], writes=[("bm_sb",)], bias=sb["bbm_s"][:, 0:1])
            if sub < 2:
                return False

            act(sb["ebx"], sb["bm_sb"][0:64, :], AF.Exp, reads=[("bm_sb",)],
                writes=[("ebx",)], scale=-1.0)
            ts("vector", sb["rbeta"], sb["ebx"], 1.0, ALU.add,
               reads=[("ebx",)], writes=[("rbeta",)], s2=1.0 / BMAX, op1=ALU.max)
            ts("vector", sb["rbeta"], sb["rbeta"], 1.0 / BMIN, ALU.min,
               reads=[("rbeta",)], writes=[("rbeta",)])
            scan(sb["invb_fm"], sb["rbeta"], sb["one64"][:], 1.0, ALU.mult, ALU.mult,
                 reads=[("rbeta",), ("one64",)], writes=[("invb_fm",)])
            recip(sb["bC_col"][:, 0:1], sb["invb_fm"][:, 127:128],
                  reads=[("invb_fm",)], writes=[("bC_col",)])

            mm(ps[7][0:32, 384:512], sb["seldiff_s"][:], sb["bm_sb"][:],
               start=True, stop=True, reads=[("seldiff_s",), ("bm_sb",)],
               writes=[("psb", 7)])
            act(sb["mw"][0:32, :], ps[7][0:32, 384:512], AF.Exp, reads=[("psb", 7)],
                writes=[("mw", 0)], scale=-1.0)
            ts("vector", sb["mw"][0:32, :], sb["mw"][0:32, :], 1.0, ALU.add,
               reads=[("mw", 0)], writes=[("mw", 0)])
            recip(sb["mix_fm"][0:32, :], sb["mw"][0:32, :], reads=[("mw", 0)],
                  writes=[("mix_fm",)])

            trans(ps[7][:, 128:192], sb["invb_fm"], IF[0:64, 0:64],
                  reads=[("invb_fm",), ("ident_f32_s",)], writes=[("psb", 7)])
            trans(ps[7][:, 192:224], sb["mix_fm"][0:32, :], IF[0:32, 0:32],
                  reads=[("mix_fm",), ("ident_f32_s",)], writes=[("psb", 7)])
            trans(ps[7][0:1, 256:320], sb["bC_col"][:, 0:1], IF[0:64, 0:64],
                  reads=[("bC_col",), ("ident_f32_s",)], writes=[("psb", 7)])
            cp("scalar", sb["invb_tm"], ps[7][:, 128:192], reads=[("psb", 7)],
               writes=[("invb_tm",)])
            cp("scalar", sb["mix_tm"][:, 0:32], ps[7][:, 192:224], reads=[("psb", 7)],
               writes=[("mix_tm",)])
            ts("vector", sb["mix_tm"][:, 32:64], sb["mix_tm"][:, 0:32], -1.0, ALU.mult,
               reads=[("mix_tm",)], writes=[("mix_tm",)], s2=1.0, op1=ALU.add)
            cp("scalar", sb["bC_row"][0:1, :], ps[7][0:1, 256:320], reads=[("psb", 7)],
               writes=[("bC_row",)])
            mm(ps[7][:, 320:384], sb["ones_f32_s"][0:1, 0:128], sb["bC_row"][0:1, :],
               start=True, stop=True, reads=[("bC_row",), ("ones_f32_s",)],
               writes=[("psb", 7)])
            cp("scalar", sb["bCb"], ps[7][:, 320:384], reads=[("psb", 7)], writes=[("bCb",)])
            ts("vector", sb["epsinvb"], sb["invb_tm"], ATT_EPS, ALU.mult,
               reads=[("invb_tm",)], writes=[("epsinvb",)])
            if sub < 3:
                return False

            for (b0, pnm) in [(0, "phiq_tm"), (2, "phik_tm")]:
                cp("scalar", sb["qke"][:, 0:512], ps[b0][:, 0:512],
                   reads=[("psb", b0)], writes=[("qke",)])
                cp("scalar", sb["qke"][:, 512:1024], ps[b0 + 1][:, 0:512],
                   reads=[("psb", b0 + 1)], writes=[("qke",)])
                E3 = sb["qke"].rearrange("p (h d) -> p h d", d=32)
                R3 = sb["rr"].rearrange("p (h d) -> p h d", d=32)
                T3 = sb["tmp_bf"].rearrange("p (h d) -> p h d", d=32)
                u1, u2 = E3[:, :, 0:16], E3[:, :, 16:32]
                tt("vector", T3[:, :, 0:16], u1, COS, ALU.mult,
                   reads=[("qke",), (ck, 0)], writes=[("tmp", 0)])
                tt("vector", T3[:, :, 16:32], u2, SIN, ALU.mult,
                   reads=[("qke",), (ck, 1)], writes=[("tmp", 1)])
                tt("vector", R3[:, :, 0:16], T3[:, :, 0:16], T3[:, :, 16:32], ALU.subtract,
                   reads=[("tmp", 0), ("tmp", 1)], writes=[("rr",)])
                tt("vector", T3[:, :, 0:16], u1, SIN, ALU.mult,
                   reads=[("qke",), (ck, 1)], writes=[("tmp", 0)])
                tt("vector", T3[:, :, 16:32], u2, COS, ALU.mult,
                   reads=[("qke",), (ck, 0)], writes=[("tmp", 1)])
                tt("vector", R3[:, :, 16:32], T3[:, :, 0:16], T3[:, :, 16:32], ALU.add,
                   reads=[("tmp", 0), ("tmp", 1)], writes=[("rr",)])
                ts("vector", sb["tmp_bf"], sb["rr"], 0.0, ALU.min,
                   reads=[("rr",)], writes=[("tmp", 0), ("tmp", 1)])
                act(sb["tmp_bf"], sb["tmp_bf"], AF.Exp,
                    reads=[("tmp", 0), ("tmp", 1)], writes=[("tmp", 0), ("tmp", 1)])
                ts("vector", sb["qke"], sb["rr"], 0.0, ALU.max,
                   reads=[("rr",)], writes=[("qke",)])
                tt("vector", sb[pnm], sb["tmp_bf"], sb["qke"], ALU.add,
                   reads=[("tmp", 0), ("tmp", 1), ("qke",)], writes=[(pnm,)])

            cp("scalar", sb["vy"][:, 0:512], ps[4][:, 0:512], reads=[("psb", 4)],
               writes=[("vy",)])
            cp("scalar", sb["vy"][:, 512:1024], ps[5][:, 0:512], reads=[("psb", 5)],
               writes=[("vy",)])

            for (pnm, fnm) in [("phiq_tm", "phiq_fm"), ("phik_tm", "phik_fm")]:
                for j in range(8):
                    trans(ps[6][:, j * 128:(j + 1) * 128], sb[pnm][:, j * 128:(j + 1) * 128],
                          IBF, reads=[(pnm,), ("ident_bf_s",)], writes=[("psb", 6)])
                cp("scalar", sb[fnm], ps[6][:], reads=[("psb", 6)], writes=[(fnm,)])

            dbg_store(f"dbg_pqf_{t}_{c}", sb["phiq_fm"], [("phiq_fm",)])
            dbg_store(f"dbg_pkf_{t}_{c}", sb["phik_fm"], [("phik_fm",)])
            dbg_store(f"dbg_pqt_{t}_{c}", sb["phiq_tm"], [("phiq_tm",)])
            VS4 = sb["vsaug"].rearrange("p (h m e) -> p h m e", m=2, e=33)
            IV3 = sb["invb_tm"].rearrange("p (m h) -> p h m", h=32)
            tt("vector", VS4[:, :, :, 0:32],
               sb["vy"].rearrange("p (h d) -> p h d", d=32).unsqueeze(2).to_broadcast([128, 32, 2, 32]),
               IV3.unsqueeze(3).to_broadcast([128, 32, 2, 32]),
               ALU.mult, reads=[("vy",), ("invb_tm",)], writes=[("vsaug",)])
            cp("vector", VS4[:, :, :, 32:33], IV3.unsqueeze(3),
               reads=[("invb_tm",)], writes=[("vsaug",)])

            dbg_store(f"dbg_vsaug_{t}_{c}", sb["vsaug"], [("vsaug",)])
            cp("vector", sb["S_bf"][:], S_all[:], reads=[("S",)], writes=[("S_bf",)])
            if sub < 4:
                return False

            for g in range(8):
                ptsb = sb["ptb0"] if g % 2 == 0 else sb["ptb1"]
                ptk = ("ptb", g % 2)
                yaps = ps[g % 2]
                yak = ("psb", g % 2)
                for w in range(2):
                    for k in (2 * w, 2 * w + 1):
                        mm(ps[2 + k % 2][:, 0:128],
                           sb["phik_fm"][:, 128 * g:128 * (g + 1)][32 * k:32 * k + 32, :],
                           sb["phiq_fm"][:, 128 * g:128 * (g + 1)][32 * k:32 * k + 32, :],
                           start=True, stop=True, tp=(32 * k, 0),
                           reads=[("phik_fm",), ("phiq_fm",)], writes=[("psb", 2 + k % 2)])
                    for k in (2 * w, 2 * w + 1):
                        stt("vector", ptsb[:, 128 * k:128 * (k + 1)],
                            ps[2 + k % 2][:, 0:128], 1.0, sb["triu_s"][:],
                            ALU.mult, ALU.mult,
                            reads=[("psb", 2 + k % 2), ("triu_s",)], writes=[ptk])
                for k in range(4):
                    h = 4 * g + k
                    mm(yaps[:, 66 * k:66 * (k + 1)], ptsb[:, 128 * k:128 * (k + 1)],
                       sb["vsaug"][:, 66 * h:66 * (h + 1)],
                       start=True, stop=False, reads=[ptk, ("vsaug",)], writes=[yak])
                    mm(yaps[:, 66 * k:66 * (k + 1)], sb["phiq_fm"][:, 128 * g:128 * (g + 1)],
                       sb["S_bf"][:, 66 * h:66 * (h + 1)],
                       start=False, stop=True, reads=[("phiq_fm",), ("S_bf",)], writes=[yak])
                dps = ps[4] if g < 4 else ps[5]
                dpk = ("psb", 4) if g < 4 else ("psb", 5)
                gg = g % 4
                if not nosd:
                    for k in range(4):
                        h = 4 * g + k
                        mm(dps[32 * k:32 * k + 32, 66 * gg:66 * (gg + 1)],
                           sb["phik_tm"][:, 32 * h:32 * h + 32],
                           sb["vsaug"][:, 66 * h:66 * (h + 1)],
                           start=True, stop=True, tp=(0, 32 * k),
                           reads=[("phik_tm",), ("vsaug",)], writes=[dpk])
                if g == 0 and f"dbg_ya_{t}_{c}" in dbg_t:
                    cp("vector", sb["aff"][:, 0:264], yaps[:, 0:264],
                       reads=[yak], writes=[("aff",)])
                    dbg_store(f"dbg_ya_{t}_{c}", sb["aff"][:, 0:264], [("aff",)])
                cp("scalar", sb["ya_sb"][:, 264 * g:264 * (g + 1)], yaps[:, 0:264],
                   reads=[yak], writes=[("ya_sb", g)])

            yakeys = [("ya_sb", g) for g in range(8)]
            YAS = sb["ya_sb"].rearrange("p (h m s) -> p h m s", m=2, s=33)
            EIA = sb["epsinvb"].rearrange("p (m h) -> p h m", h=32)
            tt("vector", sb["dsb2"].rearrange("p (h m) -> p h m", m=2).unsqueeze(3),
               YAS[:, :, :, 32:33], EIA.unsqueeze(3), ALU.add,
               reads=yakeys + [("epsinvb",)], writes=[("dsb2",)])
            recip(sb["rsb2"], sb["dsb2"], reads=[("dsb2",)], writes=[("rsb2",)])
            MXA = sb["mix_tm"].rearrange("p (m h) -> p h m", h=32)
            tt("vector", sb["wsb2"].rearrange("p (h m) -> p h m", m=2),
               sb["rsb2"].rearrange("p (h m) -> p h m", m=2), MXA, ALU.mult,
               reads=[("rsb2",), ("mix_tm",)], writes=[("wsb2",)])
            W3A = sb["wsb2"].rearrange("p (h m) -> p h m", m=2)
            tt("vector", sb["aff"][:].rearrange("p (h d) -> p h d", d=32),
               YAS[:, :, 0, 0:32],
               W3A[:, :, 0:1].to_broadcast([128, 32, 32]), ALU.mult,
               reads=yakeys + [("wsb2",)], writes=[("aff",)])
            tt("vector", sb["z_tm"][:].rearrange("p (h d) -> p h d", d=32),
               YAS[:, :, 1, 0:32],
               W3A[:, :, 1:2].to_broadcast([128, 32, 32]), ALU.mult,
               reads=yakeys + [("wsb2",)], writes=[("z_tm",)])
            tt("vector", sb["vy"][:], sb["aff"][:], sb["z_tm"][:], ALU.add,
               reads=[("aff",), ("z_tm",)], writes=[("vy",)])

            for k in range(4):
                SV = S_all[32 * k:32 * k + 32, :].rearrange(
                    "q (g kk m s) -> q g kk m s", kk=4, m=2, s=33)[:, 0:4, k, :, :]
                DV = ps[4][32 * k:32 * k + 32, 0:264].rearrange(
                    "q (g m s) -> q g m s", m=2, s=33)
                tt("vector", SV, SV, DV, ALU.add, reads=[("S",), ("psb", 4)], writes=[("S",)])
            for k in range(4):
                SV = S_all[32 * k:32 * k + 32, :].rearrange(
                    "q (g kk m s) -> q g kk m s", kk=4, m=2, s=33)[:, 4:8, k, :, :]
                DV = ps[5][32 * k:32 * k + 32, 0:264].rearrange(
                    "q (g m s) -> q g m s", m=2, s=33)
                tt("vector", SV, SV, DV, ALU.add, reads=[("S",), ("psb", 5)], writes=[("S",)])
            SV4 = S_all[:].rearrange("p (h m s) -> p h m s", m=2, s=33)
            BC3 = sb["bCb"].rearrange("p (m h) -> p h m", h=32)
            tt("vector", SV4, SV4, BC3.unsqueeze(3).to_broadcast([128, 32, 2, 33]),
               ALU.mult, reads=[("S",), ("bCb",)], writes=[("S",)])

            dbg_store(f"dbg_y_{t}_{c}", sb["vy"][:], [("vy",)])
            if sub < 6:
                return False
            return True

        def stage_W(c):
            cp("vector", sb["tl2_bf"], sb["vy"][:], reads=[("vy",)], writes=[("tl2",)])
            for j in range(8):
                trans(ps[6][:, j * 128:(j + 1) * 128], sb["tl2_bf"][:, j * 128:(j + 1) * 128],
                      IBF, reads=[("tl2",), ("ident_bf_s",)], writes=[("psb", 6)])
            cp("vector", sb["tl_fm"], ps[6][:], reads=[("psb", 6)], writes=[("tlfm",)])
            for j in range(8):
                for th in range(2):
                    mm(ps[2 + th][:, 0:512], sb["tl_fm"][:, j * 128:(j + 1) * 128],
                       sb["wo_s"][:, j * 1024 + th * 512: j * 1024 + (th + 1) * 512],
                       start=(j == 0), stop=False,
                       reads=[("tlfm",), ("wo_s",)], writes=[("psb", 2 + th)])
            for th in range(2):
                mm(ps[2 + th][:, 0:512], sb["ones_bf_s"][0:1, 0:128],
                   sb["bout_s"][0:1, th * 512:(th + 1) * 512],
                   start=False, stop=True,
                   reads=[("ones_bf_s",), ("bout_s",)], writes=[("psb", 2 + th)])
            stt("vector", sb["z_tm"][:, 0:512], ps[2][:, 0:512], 1.0, sb["a_tm"][:, 0:512],
                ALU.mult, ALU.add, reads=[("psb", 2), ("a_tm",)], writes=[("z_tm",)])
            stt("vector", sb["z_tm"][:, 512:1024], ps[3][:, 0:512], 1.0, sb["a_tm"][:, 512:1024],
                ALU.mult, ALU.add, reads=[("psb", 3), ("a_tm",)], writes=[("z_tm",)])

        def stage_Tl(c):
            xc = x_sb[:, c * H:(c + 1) * H]
            xck = ("x_sb", c)
            rstd2, nb2 = ln_stats(sb["z_tm"][:], [("z_tm",)], 32)
            act(sb["tl2_bf"], sb["z_tm"][:], AF.Identity, reads=[("z_tm",), ("st", 34), ("st", 35)],
                writes=[("tl2",)], bias=nb2, scale=rstd2)
            tt("vector", sb["tl2_bf"], sb["tl2_bf"], sb["lng_s"][:], ALU.mult,
               reads=[("tl2",), ("lng_s",)], writes=[("tl2",)])
            tt("vector", sb["tl2_bf"], sb["tl2_bf"], sb["lnb_s"][:], ALU.add,
               reads=[("tl2",), ("lnb_s",)], writes=[("tl2",)])
            rstd3, nb3 = ln_stats(sb["tl2_bf"], [("tl2",)], 16)
            act(sb["aff"][:], sb["tl2_bf"], AF.Identity, reads=[("tl2",), ("st", 18), ("st", 19)],
                writes=[("aff",)], bias=nb3, scale=rstd3)
            tt("vector", xc, xc, sb["aff"][:], ALU.add, reads=[xck, ("aff",)], writes=[xck])
            dbg_store(f"dbg_x1_{t}_{c}", xc, [xck])
            rstd4, nb4 = ln_stats(xc, [xck], 40)
            act(sb["tl_bf"], xc, AF.Identity, reads=[xck, ("st", 42), ("st", 43)],
                writes=[("tlz",)], bias=nb4, scale=rstd4)

        def stage_Z(c):
            for j in range(8):
                trans(ps[6][:, j * 128:(j + 1) * 128], sb["tl_bf"][:, j * 128:(j + 1) * 128],
                      IBF, reads=[("tlz",), ("ident_bf_s",)], writes=[("psb", 6)])
            cp("scalar",
               z1_fm[:].rearrange("p (j cc f) -> p j cc f", cc=8, f=128)[:, :, c, :],
               ps[6][:].rearrange("p (j f) -> p j f", f=128),
               reads=[("psb", 6)], writes=[("z1", c)])
            pass

        stage_A(0)
        cut = False
        for c in range(NCHUNK):
            if not stage_B(c):
                cut = True
                continue
            if c >= 1:
                stage_Z(c - 1)
            stage_W(c)
            if c < NCHUNK - 1:
                stage_A(c + 1)
            stage_Tl(c)
        if not cut:
            stage_Z(NCHUNK - 1)

        P.guard()

        if stage < 2:
            for c in range(NCHUNK):
                dma("sync", OUT[t * T_TILE + c * C: t * T_TILE + (c + 1) * C, :],
                    x_sb[:, c * H:(c + 1) * H], reads=[("x_sb", c)], writes=[("outd", t, c)])
            continue

        for c in range(NCHUNK):
            for j in range(8):
                mm(ps[7][:, 0:16], z1_fm[:, j * 1024 + c * 128: j * 1024 + (c + 1) * 128],
                   sb["wr_s"][:, j * 16:(j + 1) * 16],
                   start=(j == 0), stop=(j == 7), reads=[("z1", c), ("wr_s",)],
                   writes=[("psb", 7)])
            act(sb["e_all"][:, 16 * c:16 * (c + 1)], ps[7][:, 0:16], AF.Exp,
                reads=[("psb", 7)], writes=[("eall", c)], scale=-1.0)

        def router_chain():
            eks = [("eall", c) for c in range(NCHUNK)]
            V = lambda nm: sb[nm]
            ts("vector", V("sc_all"), V("e_all"), 1.0, ALU.add, reads=eks, writes=[("sc_all",)])
            recip(V("scores_a"), V("sc_all"), reads=[("sc_all",)], writes=[("scores_a",)])
            tt("vector", V("sc_all").rearrange("p (c e) -> p c e", e=16),
               V("scores_a").rearrange("p (c e) -> p c e", e=16),
               sb["ebias_s"][:].unsqueeze(1).to_broadcast([128, 8, 16]),
               ALU.add, reads=[("scores_a",), ("ebias_s",)], writes=[("sc_all",)])
            SC = V("sc_all").rearrange("p (c g j) -> p c g j", g=4, j=4)
            GW = V("gsw")
            g32 = lambda o: GW[:, o:o + 32].rearrange("p (c g) -> p c g", g=4)
            tt("vector", g32(0), SC[:, :, :, 0], SC[:, :, :, 1], ALU.max,
               reads=[("sc_all",)], writes=[("gsw", 0)])
            tt("vector", g32(32), SC[:, :, :, 0], SC[:, :, :, 1], ALU.min,
               reads=[("sc_all",)], writes=[("gsw", 1)])
            tt("vector", g32(64), SC[:, :, :, 2], SC[:, :, :, 3], ALU.max,
               reads=[("sc_all",)], writes=[("gsw", 2)])
            tt("vector", g32(96), SC[:, :, :, 2], SC[:, :, :, 3], ALU.min,
               reads=[("sc_all",)], writes=[("gsw", 3)])
            tt("vector", g32(128), g32(0), g32(64), ALU.max,
               reads=[("gsw", 0), ("gsw", 2)], writes=[("gsw", 4)])
            tt("vector", g32(160), g32(0), g32(64), ALU.min,
               reads=[("gsw", 0), ("gsw", 2)], writes=[("gsw", 5)])
            tt("vector", g32(192), g32(32), g32(96), ALU.max,
               reads=[("gsw", 1), ("gsw", 3)], writes=[("gsw", 6)])
            tt("vector", g32(192), g32(160), g32(192), ALU.max,
               reads=[("gsw", 5), ("gsw", 6)], writes=[("gsw", 6)])
            tt("vector", g32(0), g32(128), g32(192), ALU.add,
               reads=[("gsw", 4), ("gsw", 6)], writes=[("gs",)])
            mset("vector", GW[:, 32:64], 0.0, reads=[], writes=[("gm",)])
            GS = g32(0)
            GM = g32(32)
            for _ in range(2):
                tt("vector", V("m8a").rearrange("p (c u) -> p c u", u=2),
                   GS[:, :, 0:2], GS[:, :, 2:4], ALU.max, reads=[("gs",)], writes=[("m8a",)])
                tt("vector", V("m8b")[:, 0:8],
                   V("m8a").rearrange("p (c u) -> p c u", u=2)[:, :, 0],
                   V("m8a").rearrange("p (c u) -> p c u", u=2)[:, :, 1],
                   ALU.max, reads=[("m8a",)], writes=[("m8b",)])
                tt("vector", g32(64), GS,
                   V("m8b")[:, 0:8].unsqueeze(2).to_broadcast([128, 8, 4]),
                   ALU.is_ge, reads=[("gs",), ("m8b",)], writes=[("geq",)])
                stt("vector", g32(96), g32(64), -1e9,
                    sb["iota4_s"][:].unsqueeze(1).to_broadcast([128, 8, 4]),
                    ALU.mult, ALU.add, reads=[("geq",), ("iota4_s",)], writes=[("gcand",)])
                tt("vector", V("m8a").rearrange("p (c u) -> p c u", u=2),
                   g32(96)[:, :, 0:2], g32(96)[:, :, 2:4], ALU.min,
                   reads=[("gcand",)], writes=[("m8a",)])
                tt("vector", V("m8b")[:, 0:8],
                   V("m8a").rearrange("p (c u) -> p c u", u=2)[:, :, 0],
                   V("m8a").rearrange("p (c u) -> p c u", u=2)[:, :, 1],
                   ALU.min, reads=[("m8a",)], writes=[("m8b",)])
                tt("vector", g32(64), g32(96),
                   V("m8b")[:, 0:8].unsqueeze(2).to_broadcast([128, 8, 4]),
                   ALU.is_equal, reads=[("gcand",), ("m8b",)], writes=[("goh",)])
                tt("vector", GM, GM, g32(64), ALU.add, reads=[("gm",), ("goh",)], writes=[("gm",)])
                stt("vector", GS, g32(64), -1e9, GS, ALU.mult, ALU.add,
                    reads=[("goh",), ("gs",)], writes=[("gs",)])
            cp("vector", V("smask").rearrange("p (c g j) -> p c g j", g=4, j=4),
               GM.unsqueeze(3).to_broadcast([128, 8, 4, 4]),
               reads=[("gm",)], writes=[("smask",)])
            tt("vector", V("masked"), V("sc_all"), V("smask"), ALU.mult,
               reads=[("sc_all",), ("smask",)], writes=[("masked",)])
            MK = lambda: V("masked").rearrange("p (c e) -> p c e", e=16)
            for kk in range(4):
                tt("vector", V("ohk").rearrange("p (c e) -> p c e", e=16)[:, :, 0:8],
                   MK()[:, :, 0:8], MK()[:, :, 8:16], ALU.max,
                   reads=[("masked",)], writes=[("tr8",)])
                OH8 = V("ohk").rearrange("p (c e) -> p c e", e=16)
                tt("vector", OH8[:, :, 8:12], OH8[:, :, 0:4], OH8[:, :, 4:8], ALU.max,
                   reads=[("tr8",)], writes=[("tr4",)])
                tt("vector", V("m8a").rearrange("p (c u) -> p c u", u=2),
                   OH8[:, :, 8:10], OH8[:, :, 10:12], ALU.max,
                   reads=[("tr4",)], writes=[("m8a",)])
                tt("vector", V("m8b")[:, 0:8],
                   V("m8a").rearrange("p (c u) -> p c u", u=2)[:, :, 0],
                   V("m8a").rearrange("p (c u) -> p c u", u=2)[:, :, 1],
                   ALU.max, reads=[("m8a",)], writes=[("m8b",)])
                tt("vector", V("ohk"), V("masked"),
                   V("m8b")[:, 0:8].unsqueeze(2).to_broadcast([128, 8, 16]),
                   ALU.is_ge, reads=[("masked",), ("m8b",)], writes=[("ohk",)])
                stt("vector", V("cand").rearrange("p (c e) -> p c e", e=16),
                    V("ohk").rearrange("p (c e) -> p c e", e=16), -1e9,
                    sb["iota16_s"][:].unsqueeze(1).to_broadcast([128, 8, 16]),
                    ALU.mult, ALU.add, reads=[("ohk",), ("iota16_s",)], writes=[("cand",)])
                CD = V("cand").rearrange("p (c e) -> p c e", e=16)
                tt("vector", V("ohk").rearrange("p (c e) -> p c e", e=16)[:, :, 0:8],
                   CD[:, :, 0:8], CD[:, :, 8:16], ALU.min, reads=[("cand",)], writes=[("tr8",)])
                tt("vector", OH8[:, :, 8:12], OH8[:, :, 0:4], OH8[:, :, 4:8], ALU.min,
                   reads=[("tr8",)], writes=[("tr4",)])
                tt("vector", V("m8a").rearrange("p (c u) -> p c u", u=2),
                   OH8[:, :, 8:10], OH8[:, :, 10:12], ALU.min,
                   reads=[("tr4",)], writes=[("m8a",)])
                tt("vector", V("m8b")[:, 0:8],
                   V("m8a").rearrange("p (c u) -> p c u", u=2)[:, :, 0],
                   V("m8a").rearrange("p (c u) -> p c u", u=2)[:, :, 1],
                   ALU.min, reads=[("m8a",)], writes=[("m8b",)])
                tt("vector", V("ohk"), V("cand"),
                   V("m8b")[:, 0:8].unsqueeze(2).to_broadcast([128, 8, 16]),
                   ALU.is_equal, reads=[("cand",), ("m8b",)], writes=[("ohk",)])
                tt("vector", V("cand"), V("scores_a"), V("ohk"), ALU.mult,
                   reads=[("scores_a",), ("ohk",)], writes=[("wkc",)])
                tt("vector", CD[:, :, 0:8], CD[:, :, 0:8], CD[:, :, 8:16], ALU.add,
                   reads=[("wkc",)], writes=[("wkc",)])
                tt("vector", CD[:, :, 0:4], CD[:, :, 0:4], CD[:, :, 4:8], ALU.add,
                   reads=[("wkc",)], writes=[("wkc",)])
                tt("vector", CD[:, :, 0:2], CD[:, :, 0:2], CD[:, :, 2:4], ALU.add,
                   reads=[("wkc",)], writes=[("wkc",)])
                tt("vector", V("wk4")[:, 8 * kk:8 * kk + 8],
                   CD[:, :, 0], CD[:, :, 1], ALU.add,
                   reads=[("wkc",)], writes=[("wk4", kk)])
                stt("vector", V("masked"), V("ohk"), -1e9, V("masked"),
                    ALU.mult, ALU.add, reads=[("ohk",), ("masked",)], writes=[("masked",)])
                dst = V("combA") if kk % 2 == 0 else V("combB")
                srx = V("combB") if kk % 2 == 0 else V("combA")
                if kk == 0:
                    tt("vector", dst.rearrange("p (c e) -> p c e", e=16),
                       V("ohk").rearrange("p (c e) -> p c e", e=16),
                       V("wk4")[:, 0:8].unsqueeze(2).to_broadcast([128, 8, 16]),
                       ALU.mult, reads=[("ohk",), ("wk4", 0)], writes=[("comb2", 0)])
                else:
                    tt("vector", V("cand").rearrange("p (c e) -> p c e", e=16),
                       V("ohk").rearrange("p (c e) -> p c e", e=16),
                       V("wk4")[:, 8 * kk:8 * kk + 8].unsqueeze(2).to_broadcast([128, 8, 16]),
                       ALU.mult, reads=[("ohk",), ("wk4", kk)], writes=[("wkc",)])
                    tt("vector", dst, srx, V("cand"), ALU.add,
                       reads=[("comb2", kk - 1), ("wkc",)], writes=[("comb2", kk)])
            W4 = V("wk4")
            tt("vector", V("m8a")[:, 0:8], W4[:, 0:8], W4[:, 8:16], ALU.add,
               reads=[("wk4", 0), ("wk4", 1)], writes=[("m8a",)])
            tt("vector", V("m8b")[:, 0:8], W4[:, 16:24], W4[:, 24:32], ALU.add,
               reads=[("wk4", 2), ("wk4", 3)], writes=[("m8b",)])
            ts("vector", V("m8a")[:, 0:8], V("m8a")[:, 0:8], 1e-20, ALU.add,
               reads=[("m8a",)], writes=[("m8a",)])
            tt("vector", V("m8a")[:, 0:8], V("m8a")[:, 0:8], V("m8b")[:, 0:8], ALU.add,
               reads=[("m8a",), ("m8b",)], writes=[("m8a",)])
            recip(V("m8b")[:, 0:8], V("m8a")[:, 0:8], reads=[("m8a",)], writes=[("m8b",)])
            ts("vector", V("m8b")[:, 0:8], V("m8b")[:, 0:8], 2.5, ALU.mult,
               reads=[("m8b",)], writes=[("m8b",)])
            tt("vector", V("comb_all").rearrange("p (c e) -> p c e", e=16),
               V("combB").rearrange("p (c e) -> p c e", e=16),
               V("m8b")[:, 0:8].unsqueeze(2).to_broadcast([128, 8, 16]),
               ALU.mult, reads=[("comb2", 3), ("m8b",)], writes=[("comball",)])

        def comb_transposes():
            for c in range(NCHUNK):
                trans(ps[7][0:16, 0:128], sb["comb_all"][:, 16 * c:16 * (c + 1)],
                      IF, reads=[("comball",), ("ident_f32_s",)], writes=[("psb", 7)])
                cp("scalar", sb["comb_fm"][:, 128 * c:128 * (c + 1)], ps[7][0:16, 0:128],
                   reads=[("psb", 7)], writes=[("combfm", c)])

        if stage < 3:
            router_chain()
            comb_transposes()
            for c in range(NCHUNK):
                dma("sync", OUT[t * T_TILE + c * C: t * T_TILE + (c + 1) * C, :],
                    x_sb[:, c * H:(c + 1) * H], reads=[("x_sb", c)], writes=[("outd", t, c)])
            P.guard()
            continue

        for e in range(NE):
            gwn, uwn = f"gww{e % 2}", f"uww{e % 2}"
            dwn = f"dww{e % EGRP}"
            dma("gpsimd", sb[gwn].rearrange("p (j f) -> p j f", f=512),
                dram["gwT"][e].rearrange("(j p) f -> p j f", p=128), reads=[], writes=[(gwn,)])
            dma("gpsimd", sb[uwn].rearrange("p (j f) -> p j f", f=512),
                dram["uwT"][e].rearrange("(j p) f -> p j f", p=128), reads=[], writes=[(uwn,)])
            dma("gpsimd", sb[dwn].rearrange("p (j f) -> p j f", f=1024),
                dram["dwT"][e].rearrange("(j p) f -> p j f", p=128), reads=[], writes=[(dwn,)])
            hbuf = sb[f"h{e % EGRP}"]
            hk = ("h", e % EGRP)
            for ic in range(4):
                for th in range(2):
                    for j in range(8):
                        mm(ps[th][:, 0:512],
                           sb[gwn][:, j * 512 + ic * 128: j * 512 + (ic + 1) * 128],
                           z1_fm[:, j * 1024 + th * 512: j * 1024 + (th + 1) * 512],
                           start=(j == 0), stop=(j == 7),
                           reads=[(gwn,)] + [("z1", cc) for cc in range(th * 4, th * 4 + 4)],
                           writes=[("psb", th)])
                    for j in range(8):
                        mm(ps[2 + th][:, 0:512],
                           sb[uwn][:, j * 512 + ic * 128: j * 512 + (ic + 1) * 128],
                           z1_fm[:, j * 1024 + th * 512: j * 1024 + (th + 1) * 512],
                           start=(j == 0), stop=(j == 7),
                           reads=[(uwn,)] + [("z1", cc) for cc in range(th * 4, th * 4 + 4)],
                           writes=[("psb", 2 + th)])
                for th in range(2):
                    act(sb["s_sb"][:, th * 512:(th + 1) * 512], ps[th][:, 0:512], AF.Silu,
                        reads=[("psb", th)], writes=[("s_sb", th)])
                for th in range(2):
                    tt("vector", hbuf[:, ic * 1024 + th * 512: ic * 1024 + (th + 1) * 512],
                       sb["s_sb"][:, th * 512:(th + 1) * 512], ps[2 + th][:, 0:512],
                       ALU.mult, reads=[("s_sb", th), ("psb", 2 + th)], writes=[hk])
            if e == 0:
                router_chain()
                comb_transposes()
            if e < 16:
                dma("sync", sb["crow"][0:1, :], sb["comb_fm"][e:e + 1, :],
                    reads=[("combfm", cc) for cc in range(8)], writes=[("crow",)])
                for th in range(2):
                    mm(ps[7][:, 0:512], sb["ones_bf_s"][0:1, 0:128],
                       sb["crow"][0:1, th * 512:(th + 1) * 512],
                       start=True, stop=True,
                       reads=[("crow",), ("ones_bf_s",)],
                       writes=[("psb", 7)])
                    cp("scalar", sb["bcomb"][:, th * 512:(th + 1) * 512], ps[7][:, 0:512],
                       reads=[("psb", 7)], writes=[("bcomb", th)])
                tt("vector", hbuf[:].rearrange("p (ic f) -> p ic f", f=1024),
                   hbuf[:].rearrange("p (ic f) -> p ic f", f=1024),
                   sb["bcomb"][:].unsqueeze(1).to_broadcast([128, 4, 1024]),
                   ALU.mult, reads=[hk, ("bcomb", 0), ("bcomb", 1)], writes=[hk])
            if e % EGRP == EGRP - 1:
                grp0 = e - EGRP + 1
                first = grp0 == 0
                for fc in range(8):
                    for th in range(2):
                        pd = ps[4] if (fc * 2 + th) % 2 == 0 else ps[5]
                        pdk = ("psb", 4 + (fc * 2 + th) % 2)
                        nmm = 0
                        for ee in range(grp0, grp0 + EGRP):
                            hb = sb[f"h{ee % EGRP}"]
                            dwe = f"dww{ee % EGRP}"
                            for ic in range(4):
                                nmm += 1
                                mm(pd[:, 0:512],
                                   sb[dwe][:, ic * 1024 + fc * 128: ic * 1024 + (fc + 1) * 128],
                                   hb[:, ic * 1024 + th * 512: ic * 1024 + th * 512 + 512],
                                   start=(nmm == 1), stop=(nmm == EGRP * 4),
                                   reads=[(dwe,), ("h", ee % EGRP)], writes=[pdk])
                        dst = sb["routed"][:, fc * 1024 + th * 512: fc * 1024 + (th + 1) * 512]
                        if first:
                            cp("vector", dst, pd[:, 0:512], reads=[pdk],
                               writes=[("routed", fc, th)])
                        else:
                            stt("vector", dst, pd[:, 0:512], 1.0, dst, ALU.mult, ALU.add,
                                reads=[pdk, ("routed", fc, th)], writes=[("routed", fc, th)])

        if stage < 4:
            for c in range(NCHUNK):
                dma("sync", OUT[t * T_TILE + c * C: t * T_TILE + (c + 1) * C, :],
                    x_sb[:, c * H:(c + 1) * H], reads=[("x_sb", c)], writes=[("outd", t, c)])
            P.guard()
            continue

        for c in range(NCHUNK):
            xc = x_sb[:, c * H:(c + 1) * H]
            xck = ("x_sb", c)
            for half in range(2):
                for jj in range(4):
                    j = half * 4 + jj
                    trans(ps[7][:, jj * 128:(jj + 1) * 128],
                          sb["routed"][:, j * 1024 + c * 128: j * 1024 + (c + 1) * 128],
                          IF, reads=[("routed", j, c // 4), ("ident_f32_s",)],
                          writes=[("psb", 7)])
                cp("scalar", sb["r_tm"][:, half * 512:(half + 1) * 512], ps[7][:, 0:512],
                   reads=[("psb", 7)], writes=[("r_tm", half)])
            rstd5, nb5 = ln_stats(sb["r_tm"], [("r_tm", 0), ("r_tm", 1)], 44)
            act(sb["r_tm"], sb["r_tm"], AF.Identity,
                reads=[("r_tm", 0), ("r_tm", 1), ("st", 46), ("st", 47)],
                writes=[("r_tm", 0), ("r_tm", 1)], bias=nb5, scale=rstd5)
            tt("vector", xc, xc, sb["r_tm"], ALU.add,
               reads=[xck, ("r_tm", 0), ("r_tm", 1)], writes=[xck])
            dma("sync", OUT[t * T_TILE + c * C: t * T_TILE + (c + 1) * C, :], xc,
                reads=[xck], writes=[("outd", t, c)])

        P.guard()

    P.finalize()
    return nc, P, dbg_t


def _run(nc, P, in_maps, core_ids):
    with (
        nc.semaphore("sem_t") as s_t,
        nc.semaphore("sem_v") as s_v,
        nc.semaphore("sem_s") as s_s,
        nc.semaphore("sem_y") as s_y,
        nc.semaphore("sem_g") as s_g,
        nc.Block() as blk,
    ):
        sems = {"tensor": s_t, "vector": s_v, "scalar": s_s, "sync": s_y, "gpsimd": s_g}

        def runner(eng):
            def go(E):
                for op in P.ops[eng]:
                    for (e2, tv) in op[4]:
                        E.wait_ge(sems[e2], tv)
                    hnd = op[0](E)
                    if op[3]:
                        hnd.then_inc(sems[eng], 16 if op[2] else 1)
                if eng == "sync":
                    for e2 in ENGINES:
                        tot = P.total(e2)
                        if tot:
                            E.wait_ge(sems[e2], tot)
            return go

        blk.tensor(runner("tensor"))
        blk.vector(runner("vector"))
        blk.scalar(runner("scalar"))
        blk.sync(runner("sync"))
        blk.gpsimd(runner("gpsimd"))

    return run_bass_kernel_spmd(nc, in_maps, core_ids=core_ids)


def kernel(**inputs):
    host = _prep_host(inputs)
    x = np.ascontiguousarray(np.asarray(inputs["L0_attn_x"], dtype=np.float32))
    nc, P, _ = build_nc(n_tiles=4)
    in_maps = []
    for core in range(8):
        m = dict(host)
        m["x"] = x[2 * core:2 * core + 2].reshape(4096, H)
        in_maps.append(m)
    res = _run(nc, P, in_maps, core_ids=list(range(8)))
    out = np.empty((16, 2048, H), np.float32)
    for core in range(8):
        out[2 * core:2 * core + 2] = res.results[core]["out"].reshape(2, 2048, H)
    return out
